# version 21
# baseline (speedup 1.0000x reference)
import contextlib

import numpy as np
import ml_dtypes

import concourse.bass as bass
import concourse.mybir as mybir
from concourse.bass_utils import run_bass_kernel_spmd

F32 = mybir.dt.float32
BF16 = mybir.dt.bfloat16
AF = mybir.ActivationFunctionType
ALU = mybir.AluOpType
AX = mybir.AxisListType

B, T, C = 4, 4096, 1024
NH, NL, HS = 16, 64, 64
FF = 4096
NEXP = 8
TOK = 2048
NTC = TOK // 128
NSC = 2 * NTC
NDC = C // 128
NPAIR = NH // 2
NFC = FF // 128
NCH = 4
CHT = TOK // NCH
N_ITER = 6
EPS = 1e-5

BF = ml_dtypes.bfloat16


def _host_consts(h):
    c = {}
    A = np.zeros((128, NSC, NL), np.float32)
    for i in range(NSC):
        half = h if i < NTC else 1 - h
        base = 32 * half + 2 * (i % NTC)
        A[0:64, i, base] = 1.0 / 64.0
        A[64:128, i, base + 1] = 1.0 / 64.0
    c["A_all"] = np.ascontiguousarray(A.astype(BF))
    c["eye64b"] = np.eye(64, dtype=BF)
    c["eye8f"] = np.eye(8, dtype=np.float32)
    eyebd = np.zeros((128, 128), np.float32)
    eyebd[0:64, 0:64] = np.eye(64)
    eyebd[64:128, 64:128] = np.eye(64)
    c["eye7bd"] = 7.0 * eyebd
    c["eye15bd"] = 15.0 * eyebd
    c["eye13bd"] = 13.0 * eyebd
    c["eye128f"] = np.eye(128, dtype=np.float32)
    oc2 = np.zeros((128, 2), np.float32)
    oc2[0:64, 0] = 1.0
    oc2[64:128, 1] = 1.0
    c["onescol2b"] = oc2.astype(BF)
    c["onescol2f"] = oc2
    sel = np.zeros((2, 128), np.float32)
    sel[0, 0:64] = 1.0
    sel[1, 64:128] = 1.0
    c["sel2f"] = sel
    c["ones1f"] = np.ones((1, 128), np.float32)
    return c


_CONST_SPECS = [
    ("eye64b", [64, 64], BF16),
    ("eye8f", [8, 8], F32),
    ("eye7bd", [128, 128], F32),
    ("eye15bd", [128, 128], F32),
    ("eye13bd", [128, 128], F32),
    ("eye128f", [128, 128], F32),
    ("onescol2b", [128, 2], BF16),
    ("onescol2f", [128, 2], F32),
    ("sel2f", [2, 128], F32),
    ("ones1f", [1, 128], F32),
]


class Ctr:

    def __init__(self, sem):
        self.sem = sem
        self.n = 0

    def bump(self, instr, k=1):
        instr.then_inc(self.sem, k)
        self.n += k
        return (self.sem, self.n)

    def tok(self):
        return (self.sem, self.n)


def _w(eng, *toks):
    for tok in toks:
        if tok is not None and tok[1] > 0:
            eng.wait_ge(tok[0], tok[1])


DEFER_P = True
DEFER_D3 = True


def build_nc(taps=False):
    nc = bass.Bass("TRN2", target_bir_lowering=False, debug=False,
                   detect_race_conditions=False)

    x_own = nc.dram_tensor("x_own", [TOK, C], F32, kind="ExternalInput").ap()
    x_par = nc.dram_tensor("x_par", [TOK, C], F32, kind="ExternalInput").ap()
    W_attn = nc.dram_tensor("W_attn", [C, 3 * C], F32, kind="ExternalInput").ap()
    W_proj = nc.dram_tensor("W_proj", [C, C], F32, kind="ExternalInput").ap()
    WrT = nc.dram_tensor("WrT", [C, NEXP], F32, kind="ExternalInput").ap()
    W1 = nc.dram_tensor("W1", [C, FF], F32, kind="ExternalInput").ap()
    b1 = nc.dram_tensor("b1", [128, NFC], F32, kind="ExternalInput").ap()
    W2 = nc.dram_tensor("W2", [FF, C], F32, kind="ExternalInput").ap()
    b2 = nc.dram_tensor("b2", [C], F32, kind="ExternalInput").ap()
    out_h = nc.dram_tensor("out", [TOK, C], F32, kind="ExternalOutput").ap()
    scr_W1 = nc.dram_tensor("scr_W1", [8, 128, NDC, 512], BF16,
                            kind="Internal").ap()
    scr_Wp = nc.dram_tensor("scr_Wp", [128, NDC, 1024], BF16,
                            kind="Internal").ap()
    scr_W2 = nc.dram_tensor("scr_W2", [NFC, 128, C], BF16,
                            kind="Internal").ap()
    consts = {n: nc.dram_tensor(n, s, d, kind="ExternalInput").ap()
              for n, s, d in _CONST_SPECS}
    A_h = nc.dram_tensor("A_all", [128, NSC, NL], BF16,
                         kind="ExternalInput").ap()

    es = contextlib.ExitStack()
    with es:
        qG = Ctr(es.enter_context(nc.semaphore("qG")))
        qS = Ctr(es.enter_context(nc.semaphore("qS")))
        qC = Ctr(es.enter_context(nc.semaphore("qC")))
        qWs = [Ctr(es.enter_context(nc.semaphore(f"qW{i}"))) for i in range(2)]
        qF1 = [Ctr(es.enter_context(nc.semaphore(f"qF1{i}"))) for i in range(2)]
        qF2 = [Ctr(es.enter_context(nc.semaphore(f"qF2{i}"))) for i in range(2)]
        qX = [Ctr(es.enter_context(nc.semaphore(f"qX{i}"))) for i in range(2)]
        qST = Ctr(es.enter_context(nc.semaphore("qST")))
        qV = Ctr(es.enter_context(nc.semaphore("qV")))
        sPE = Ctr(es.enter_context(nc.semaphore("sPE")))
        sDV = Ctr(es.enter_context(nc.semaphore("sDV")))
        sAC = Ctr(es.enter_context(nc.semaphore("sAC")))
        sPL = Ctr(es.enter_context(nc.semaphore("sPL")))
        sEX = Ctr(es.enter_context(nc.semaphore("sEX")))

        psA = es.enter_context(nc.psum_tensor("psA", [128, 512], F32))
        psB = es.enter_context(nc.psum_tensor("psB", [128, 512], F32))
        psC = es.enter_context(nc.psum_tensor("psC", [128, 512], F32))
        psD = es.enter_context(nc.psum_tensor("psD", [128, 512], F32))

        c_sb = {}
        for name, shape, dt in _CONST_SPECS:
            c_sb[name] = es.enter_context(nc.sbuf_tensor("c_" + name, shape, dt))
        yT = es.enter_context(nc.sbuf_tensor("yT", [128, NPAIR, TOK], BF16))
        xlF = es.enter_context(nc.sbuf_tensor("xlF", [128, NDC, NL], BF16))
        wr_sb = es.enter_context(nc.sbuf_tensor("wr_sb", [128, NDC, NEXP], BF16))
        b1_sb = es.enter_context(nc.sbuf_tensor("b1_sb", [128, NFC], F32))

        block = es.enter_context(nc.Block())

        def vec(f):
            block.vector(f)

        def act(f):
            block.scalar(f)
        bar = Ctr(es.enter_context(nc.semaphore("bar")))
        tap_list = []

        def tap(name, sb_ap, shape, dtype, wait_toks):
            if not taps:
                return
            hd = nc.dram_tensor("dbg_" + name, shape, dtype,
                                kind="ExternalOutput").ap()

            def ftap(s, hd=hd, sb_ap=sb_ap, wait_toks=wait_toks):
                _w(s, *wait_toks)
                qC.bump(s.dma_start(out=hd, in_=sb_ap), 16)
            act(ftap)
            tok = qC.tok()

            def fhold(e, tok=tok):
                _w(e, tok)
            vec(fhold)
            act(fhold)
            block.tensor(fhold)
            tap_list.append(name)

        def barrier():
            target = bar.n + 5

            def fb(e):
                if hasattr(e, "drain"):
                    bar.bump(e.drain())
                else:
                    bar.bump(e.nop())
                e.wait_ge(bar.sem, target)
            block.tensor(fb)
            block.vector(fb)
            block.scalar(fb)

            def fq(e):
                bar.bump(e.nop())
                e.wait_ge(bar.sem, target)
            block.sync(fq)
            block.gpsimd(fq)

        def f(s):
            for name, _, _ in _CONST_SPECS:
                qC.bump(s.dma_start(out=c_sb[name][:], in_=consts[name][:]), 16)
            qC.bump(s.dma_start(out=b1_sb[:], in_=b1[:]), 16)
        act(f)
        t_consts = qC.tok()

        attn = contextlib.ExitStack()
        attn.__enter__()
        xT = attn.enter_context(nc.sbuf_tensor("xT", [128, NSC, NDC, 128], BF16))

        early = contextlib.ExitStack()
        early.__enter__()
        xall = early.enter_context(nc.sbuf_tensor("xall", [128, NSC, C], BF16))
        xl_sb = early.enter_context(nc.sbuf_tensor("xl_sb", [64, C], BF16))
        psXL = early.enter_context(nc.psum_tensor("psXL", [64, C], F32))
        psTbE = early.enter_context(nc.psum_tensor("psTbE", [128, 64], BF16))
        A_sb = early.enter_context(nc.sbuf_tensor("A_sb", [128, NSC, NL], BF16))

        def f(s):
            qC.bump(s.dma_start(out=A_sb[:], in_=A_h[:]), 16)
        act(f)
        t_A = qC.tok()

        def f(g):
            qG.bump(g.dma_start(
                out=xall[:, 0:NTC, :],
                in_=x_own.rearrange("(tc tp) d -> tp tc d", tp=128)), 16)
        block.gpsimd(f)
        t_xown = qG.tok()

        def f(g):
            qG.bump(g.dma_start(
                out=xall[:, NTC:NSC, :],
                in_=x_par.rearrange("(tc tp) d -> tp tc d", tp=128)), 16)
            qG.bump(g.dma_start(
                out=wr_sb[:], in_=WrT.rearrange("(c p) e -> p c e", p=128)), 16)
        block.gpsimd(f)
        t_xall = (qG.sem, 48)

        def f(sy):
            _w(sy, t_xown)
            qS.bump(sy.dma_start(out=xT[:, 0:NTC, :, :],
                                 in_=xall[:, 0:NTC, :], transpose=True), 16)
        block.sync(f)
        t_xT_own = qS.tok()

        def f(sy):
            _w(sy, t_xall)
            qS.bump(sy.dma_start(out=xT[:, NTC:NSC, :, :],
                                 in_=xall[:, NTC:NSC, :], transpose=True), 16)
        block.sync(f)
        t_xT = qS.tok()

        def f(t):
            _w(t, t_xall, t_consts, t_A)
            for half in range(2):
                for i in range(NSC):
                    ins = t.matmul(psXL[:, 512 * half:512 * (half + 1)],
                                   A_sb[:, i, :],
                                   xall[:, i, 512 * half:512 * (half + 1)],
                                   start=(i == 0), stop=(i == NSC - 1))
                sPE.bump(ins)
        block.tensor(f)
        t_xlmm = sPE.tok()

        def f(v):
            _w(v, t_xlmm)
            sDV.bump(v.tensor_copy(xl_sb[:], psXL[:]))
        vec(f)
        t_xl = sDV.tok()

        t_tr_ev = None
        for dc in range(NDC):
            def f(t, dc=dc, t_tr_ev=t_tr_ev):
                _w(t, t_xl, t_tr_ev)
                sPE.bump(t.transpose(psTbE[:],
                                     xl_sb[:, 128 * dc:128 * (dc + 1)],
                                     c_sb["eye64b"][:]))
            block.tensor(f)
            t_tr = sPE.tok()

            def f(v, dc=dc, t_tr=t_tr):
                _w(v, t_tr)
                sDV.bump(v.tensor_copy(xlF[:, dc, :], psTbE[:]))
            vec(f)
            t_tr_ev = sDV.tok()
        t_xlF = sDV.tok()

        def f(g):
            _w(g, t_xT)
        block.gpsimd(f)
        barrier()
        early.__exit__(None, None, None)

        psI1 = attn.enter_context(nc.psum_tensor("psI1", [128, 4, 128], F32))
        psI2 = attn.enter_context(nc.psum_tensor("psI2", [128, 4, 128], F32))
        psRow = attn.enter_context(nc.psum_tensor("psRow", [8, 512], F32))
        psMisc = attn.enter_context(nc.psum_tensor("psMisc", [128, 512], F32))
        psE = psMisc[:, 0:128]
        psSm = psMisc[:, 128:192]
        psT128 = psMisc[:, 192:320]
        wq = [attn.enter_context(nc.sbuf_tensor(f"wq{i}", [128, NDC, 128], BF16))
              for i in range(2)]
        wk = [attn.enter_context(nc.sbuf_tensor(f"wk{i}", [128, NDC, 128], BF16))
              for i in range(2)]
        wv = [attn.enter_context(nc.sbuf_tensor(f"wv{i}", [128, NDC, 128], BF16))
              for i in range(2)]
        qTp = attn.enter_context(nc.sbuf_tensor("qTp", [128, TOK], BF16))
        kTp = attn.enter_context(nc.sbuf_tensor("kTp", [128, T], BF16))
        vTp = attn.enter_context(nc.sbuf_tensor("vTp", [128, T], BF16))
        vtok = attn.enter_context(nc.sbuf_tensor("vtok", [128, NSC, 128], BF16))
        Elm = attn.enter_context(nc.sbuf_tensor("Elm", [128, T], BF16))
        Etok = attn.enter_context(nc.sbuf_tensor("Etok", [128, NSC, 128], BF16))
        ELs = attn.enter_context(nc.sbuf_tensor("ELs", [128, NPAIR, TOK], BF16))
        bdq = attn.enter_context(nc.sbuf_tensor("bdq", [128, NPAIR, 128], BF16))
        bdk = attn.enter_context(nc.sbuf_tensor("bdk", [128, NPAIR, 128], BF16))
        zun = attn.enter_context(nc.sbuf_tensor("zun", [128, NPAIR, 128], BF16))
        ssum8 = attn.enter_context(nc.sbuf_tensor("ssum8", [128, 8], F32))
        sinv = attn.enter_context(nc.sbuf_tensor("sinv", [128, NPAIR], F32))
        d_sb = attn.enter_context(nc.sbuf_tensor("d_sb", [2, 256], F32))
        dinv = attn.enter_context(nc.sbuf_tensor("dinv", [2, 256], F32))
        K4 = attn.enter_context(nc.sbuf_tensor("K4", [128, 4, 128], F32))
        TK4 = attn.enter_context(nc.sbuf_tensor("TK4", [128, 4, 128], BF16))
        kred = attn.enter_context(nc.sbuf_tensor("kred", [128, 8], F32))
        cden = attn.enter_context(nc.sbuf_tensor("cden", [2, 4], F32))
        denominv = attn.enter_context(nc.sbuf_tensor("denominv", [128, 4], F32))
        Vg = attn.enter_context(nc.sbuf_tensor("Vg", [128, 4, 128], BF16))
        TVg = attn.enter_context(nc.sbuf_tensor("TVg", [128, 4, 128], BF16))
        Tg = attn.enter_context(nc.sbuf_tensor("Tg", [128, 4, 128], BF16))
        TP1g = attn.enter_context(nc.sbuf_tensor("TP1g", [128, 4, 128], BF16))
        PNV = attn.enter_context(nc.sbuf_tensor("PNV", [128, NPAIR, 128], BF16))

        def f(v):
            v.memset(bdq[:], 0.0)
            v.memset(bdk[:], 0.0)
            v.memset(PNV[:], 0.0)
            v.memset(K4[:], 0.0)
            sDV.bump(v.drain())
        vec(f)
        t_bdz = sDV.tok()

        rot4 = [psA, psB, psC, psD]
        t_R4 = [None, None, None, None]
        rr = [0]

        def nextbank():
            bi = rr[0] % 4
            rr[0] += 1
            return bi
        t_mix_free = None
        t_psE_free_tok = [None]
        t_psRow_free = None
        t_psI1_free = None
        t_psI2_free = None
        t_Etok_tr_prev = None
        t_vtr_prev = None
        t_zu_prev = None
        t_stripe = [None] * NPAIR
        t_stripe_free = [None] * NPAIR
        t_ELs_done = [None] * NPAIR
        t_sinv = [None] * NPAIR
        t_bd_grp = [None, None]
        t_pnv = [None, None]
        t_yev = None

        def load_stripes(p):
            def f(g, p=p):
                if p >= 2:
                    _w(g, t_stripe_free[p - 2])
                buf = p % 2
                for w_sb, col0 in ((wq, 128 * p), (wk, C + 128 * p),
                                   (wv, 2 * C + 128 * p)):
                    src = W_attn[:, col0:col0 + 128]
                    t_stripe[p] = qWs[buf].bump(g.dma_start(
                        out=w_sb[buf][:],
                        in_=src.rearrange("(dc dp) f -> dp dc f", dp=128)), 16)
            block.gpsimd(f)

        load_stripes(0)
        load_stripes(1)

        dbg1 = attn.enter_context(nc.sbuf_tensor("dbg1", [128, 512], F32)) if taps else None

        def proj_block(p, w_sb, n_groups, dst, with_lm, extra_wait, evac,
                       hook=None, evac_toks=None):
            buf = p % 2
            ctr = sDV if evac == "v" else sAC
            for g in range(n_groups):
                if hook is not None:
                    hook()
                bi = nextbank()

                def f(t, g=g, bi=bi):
                    _w(t, t_R4[bi])
                    if g == 0:
                        _w(t, t_stripe[p], t_xT_own, t_xlF, *extra_wait)
                    if g == 4:
                        _w(t, t_xT)
                    ps = rot4[bi]
                    for dc in range(NDC):
                        ins = t.matmul(ps[:], w_sb[buf][:, dc, :],
                                       xT[:, 4 * g:4 * g + 4, dc, :],
                                       start=(dc == 0), stop=(dc == NDC - 1))
                    sPE.bump(ins)
                block.tensor(f)
                mm_tok = sPE.tok()

                def fe(e, g=g, bi=bi, mm_tok=mm_tok):
                    _w(e, mm_tok)
                    if evac == "v":
                        ins = e.tensor_copy(dst[:, 512 * g:512 * (g + 1)],
                                            rot4[bi][:])
                    else:
                        ins = e.copy(dst[:, 512 * g:512 * (g + 1)],
                                     rot4[bi][:])
                    t_R4[bi] = ctr.bump(ins)
                    if evac_toks is not None:
                        evac_toks.append(ctr.tok())
                (vec if evac == "v" else act)(fe)
            if with_lm:
                def f(t):
                    _w(t, t_mix_free)
                    for dc in range(NDC):
                        ins = t.matmul(psSm, w_sb[buf][:, dc, :],
                                       xlF[:, dc, :],
                                       start=(dc == 0), stop=(dc == NDC - 1))
                    sPE.bump(ins)
                block.tensor(f)
            return sPE.tok(), ctr.tok()

        t_minv_mm = None
        pending_d = [None]
        pending_p = [None]
        pending_d7 = [None]
        zgate1 = [True]
        t_K4_free = None

        t_scr = [None]

        t_scrp = [None]

        for p in range(NPAIR):
            if p == 1:
                def f(g):
                    t_scrp[0] = qG.bump(g.dma_start(
                        out=scr_Wp,
                        in_=W_proj.rearrange("(dc dp) f -> dp dc f",
                                             dp=128)), 16)
                    for si in range(8):
                        qG.bump(g.dma_start(
                            out=scr_W1[si],
                            in_=W1[:, 512 * si:512 * (si + 1)].rearrange(
                                "(dc dp) f -> dp dc f", dp=128)), 16)
                    t_scr[0] = qG.bump(g.dma_start(
                        out=scr_W2,
                        in_=W2.rearrange("(st dp) f -> st dp f", dp=128)), 16)
                block.gpsimd(f)

            gen_d = pending_d[0]
            pending_d[0] = None

            def hook_d(gen_d=gen_d):
                if gen_d is not None and next(gen_d, "END") != "END":
                    return
                if pending_p[0] is not None:
                    next(pending_p[0], None)

            t_qlmm, t_qevac = proj_block(p, wq, 4, qTp, True, [], "v",
                                         hook=hook_d)

            def f(v, p=p, t_qlmm=t_qlmm):
                _w(v, t_qlmm, t_bdz)
                v.tensor_copy(bdq[0:64, p, 0:64], psSm[0:64])
                sDV.bump(v.tensor_copy(bdq[64:128, p, 64:128], psSm[64:128]))
            vec(f)
            t_mix_free = sDV.tok()

            t_klmm, t_kevac = proj_block(p, wk, 8, kTp, True, [], "v",
                                         hook=hook_d)

            def f(v, p=p, t_klmm=t_klmm):
                _w(v, t_klmm)
                v.tensor_copy(bdk[0:64, p, 0:64], psSm[0:64])
                sDV.bump(v.tensor_copy(bdk[64:128, p, 64:128], psSm[64:128]))
            vec(f)
            t_bdk_p = sDV.tok()
            t_mix_free = t_bdk_p
            if p in (3, 7):
                t_bd_grp[p // 4] = t_bdk_p
            if p == 7:
                if pending_p[0] is not None:
                    for _ in pending_p[0]:
                        pass
                zgate1[0] = False
                pending_p[0] = emit_pphase(grp=1, g0=4, zun_gate=zgate1)

            vevac_toks = []
            _, t_vevac = proj_block(
                p, wv, 8, vTp, False,
                [t_vtr_prev] if t_vtr_prev else [], "v", hook=hook_d,
                evac_toks=vevac_toks)
            if gen_d is not None:
                for _ in gen_d:
                    pass
            t_stripe_free[p] = sPE.tok()
            if p + 2 < NPAIR:
                load_stripes(p + 2)

            vtok_toks = []

            def f(s, p=p, vevac_toks=list(vevac_toks)):
                _w(s, t_zu_prev)
                for g in range(8):
                    _w(s, vevac_toks[g])
                    qV.bump(s.dma_start(out=vtok[:, 4 * g:4 * g + 4, :],
                                        in_=vTp[:, 512 * g:512 * (g + 1)],
                                        transpose=True), 16)
                    vtok_toks.append(qV.tok())
            act(f)
            t_vtok_tr = qV.tok()
            t_vtr_prev = t_vtok_tr

            psI1f = psI1[:].rearrange("p a b -> p (a b)")
            psI2f = psI2[:].rearrange("p a b -> p (a b)")
            use6 = p < 4
            nbase = sEX.n
            nmm_toks = []
            nbanks = []
            for g in range(8):
                if use6 and g == 4:
                    bnk, told = psI1f, t_psI1_free
                    t_psI1_free = (sEX.sem, nbase + g + 1)
                elif use6 and g == 5:
                    bnk, told = psI2f, t_psI2_free
                    t_psI2_free = (sEX.sem, nbase + g + 1)
                else:
                    bi = nextbank()
                    bnk, told = rot4[bi][:], t_R4[bi]
                    t_R4[bi] = (sEX.sem, nbase + g + 1)
                nbanks.append(bnk)

                def f(t, p=p, g=g, bnk=bnk, told=told,
                      t_kevac=t_kevac, t_bdk_p=t_bdk_p):
                    _w(t, told)
                    if g == 0:
                        _w(t, t_kevac, t_bdk_p)
                    sPE.bump(t.matmul(bnk, bdq[:, p, :],
                                      kTp[:, 512 * g:512 * (g + 1)],
                                      start=True, stop=True))
                block.tensor(f)
                nmm_toks.append(sPE.tok())
                hook_d()
            t_Nmm = sPE.tok()

            def f(s, p=p, nmm_toks=list(nmm_toks),
                  t_Etok_tr_prev=t_Etok_tr_prev, nbanks=list(nbanks)):
                for g in range(8):
                    _w(s, nmm_toks[g])
                    if g == 0:
                        _w(s, t_Etok_tr_prev)
                    sEX.bump(s.activation(
                        Elm[:, 512 * g:512 * (g + 1)], nbanks[g],
                        AF.Exp, accum_out=ssum8[:, g:g + 1]))
            act(f)
            assert sEX.n == nbase + 8
            t_nexp = sEX.tok()

            def f(v, p=p, t_nexp=t_nexp):
                _w(v, t_nexp)
                v.tensor_reduce(sinv[:, p:p + 1], ssum8[:], AX.X, ALU.add)
                v.drain()
                sDV.bump(v.reciprocal(sinv[:, p:p + 1], sinv[:, p:p + 1]))
            vec(f)
            t_sinv[p] = sDV.tok()

            etok_toks = []

            def f(sy, p=p, nbase=nbase):
                _w(sy, t_zu_prev)
                for g in range(8):
                    _w(sy, (sEX.sem, nbase + g + 1))
                    qS.bump(sy.dma_start(out=Etok[:, 4 * g:4 * g + 4, :],
                                         in_=Elm[:, 512 * g:512 * (g + 1)],
                                         transpose=True), 16)
                    etok_toks.append(qS.tok())
            block.sync(f)
            t_Etok_tr = qS.tok()
            t_Etok_tr_prev = t_Etok_tr

            lbase = sEX.n
            lmm_toks = []
            lbis = []
            for g in range(4):
                bi = nextbank()
                lbis.append(bi)

                def f(t, p=p, g=g, bi=bi, t_qevac=t_qevac, t_bdk_p=t_bdk_p):
                    _w(t, t_R4[bi])
                    if g == 0:
                        _w(t, t_qevac, t_bdk_p)
                    sPE.bump(t.matmul(rot4[bi][:], bdk[:, p, :],
                                      qTp[:, 512 * g:512 * (g + 1)],
                                      start=True, stop=True))
                block.tensor(f)
                lmm_toks.append(sPE.tok())
                t_R4[bi] = (sEX.sem, lbase + g + 1)
                hook_d()
            t_Lmm = sPE.tok()

            def f(s, p=p, lmm_toks=list(lmm_toks), lbis=list(lbis)):
                for g in range(4):
                    _w(s, lmm_toks[g])
                    sEX.bump(s.activation(
                        ELs[:, p, 512 * g:512 * (g + 1)], rot4[lbis[g]][:],
                        AF.Exp))
            act(f)
            assert sEX.n == lbase + 4
            t_Lexp = sEX.tok()

            def emit_dchain(p=p, t_Lexp=t_Lexp):
              nonlocal t_psRow_free, t_minv_mm, t_psI1_free, t_psI2_free
              psIbc = [psI1, psI2]
              for g in range(8):
                  yield
                  cols = slice(256 * g, 256 * (g + 1))

                  def f(t, p=p, g=g, cols=cols, t_Lexp=t_Lexp):
                      _w(t, t_Lexp, t_psRow_free)
                      sPE.bump(t.matmul(psRow[0:2, 0:256], c_sb["onescol2b"][:],
                                        ELs[:, p, cols], start=True, stop=True))
                  block.tensor(f)
                  mm_tok = sPE.tok()

                  def f(v, g=g, mm_tok=mm_tok, t_minv_mm=t_minv_mm):
                      _w(v, mm_tok, t_minv_mm)
                      v.drain()
                      v.tensor_copy(d_sb[:], psRow[0:2, 0:256])
                      v.drain()
                      sDV.bump(v.reciprocal(dinv[:], d_sb[:]))
                  vec(f)
                  t_psRow_free = sDV.tok()
                  t_dinv = sDV.tok()

                  yield
                  bnk = psIbc[g % 2]
                  t_bcf = t_psI1_free if g % 2 == 0 else t_psI2_free

                  def f(t, g=g, bnk=bnk, t_dinv=t_dinv, t_bcf=t_bcf):
                      _w(t, t_dinv, t_bcf)
                      sPE.bump(t.matmul(bnk[:, 0:2, :],
                                        c_sb["sel2f"][:],
                                        dinv[:], start=True, stop=True))
                  block.tensor(f)
                  t_minv_mm = sPE.tok()

                  def f(v, p=p, g=g, cols=cols, bnk=bnk, mm_tok=t_minv_mm):
                      _w(v, mm_tok)
                      sl = ELs[:, p, cols]
                      sDV.bump(
                          v.tensor_tensor(sl, sl,
                                          bnk[:, 0:2, :].rearrange("p a b -> p (a b)"),
                                          ALU.mult))
                  vec(f)
                  if g % 2 == 0:
                      t_psI1_free = sDV.tok()
                  else:
                      t_psI2_free = sDV.tok()
              t_ELs_done[p] = sDV.tok()

            if p == 3 and not DEFER_D3:
                for _ in emit_dchain():
                    pass
            elif p == 7:
                pending_d7[0] = emit_dchain()
            else:
                pending_d[0] = emit_dchain()

            def f(t, p=p, etok_toks=list(etok_toks),
                  vtok_toks=list(vtok_toks)):
                _w(t, t_mix_free)
                for tc in range(NSC):
                    if tc % 4 == 0:
                        _w(t, etok_toks[tc // 4], vtok_toks[tc // 4])
                    ins = t.matmul(psE, Etok[:, tc, :], vtok[:, tc, :],
                                   start=(tc == 0), stop=(tc == NSC - 1))
                sPE.bump(ins)
            block.tensor(f)
            t_zu_prev = sPE.tok()

            def f(v, p=p, t_zu=t_zu_prev):
                _w(v, t_zu, t_sinv[p])
                v.drain()
                sDV.bump(v.tensor_scalar(zun[:, p, :], psE,
                                         sinv[:, p:p + 1], None, ALU.mult))
            vec(f)
            t_mix_free = sDV.tok()
            t_psE_free_tok[0] = t_mix_free
            zgate1[0] = True

            if p == 0:
                tap("xT_early", xT[:], [128, NSC, NDC, 128], BF16, [t_qevac])
                tap("qTp0", qTp[:], [128, TOK], BF16, [t_qevac])
                tap("kTp0", kTp[:], [128, T], BF16, [t_kevac])
                tap("vTp0", vTp[:], [128, T], BF16, [t_vevac])
                tap("Elm0", Elm[:], [128, T], BF16, [t_nexp])
                tap("Etok0", Etok[:], [128, NSC, 128], BF16, [t_zu_prev])
                tap("vtok0", vtok[:], [128, NSC, 128], BF16, [t_zu_prev])
                tap("sinv0", sinv[:, 0:1], [128, 1], F32, [t_sinv[p]])
                tap("zun0", zun[:, 0, :], [128, 128], BF16, [t_mix_free])
                tap("ELs0", ELs[:, 0, :], [128, TOK], BF16, [t_ELs_done[0]])
                tap("bd0q", bdq[:], [128, NPAIR, 128], BF16, [t_bdk_p])
                tap("bd0k", bdk[:], [128, NPAIR, 128], BF16, [t_bdk_p])

            if p not in (3, 7):
                continue
            grp = p // 4
            g0 = 4 * grp

            def emit_pphase(grp=grp, g0=g0, zun_gate=None):
              nonlocal t_psI1_free, t_psI2_free, t_psRow_free, t_mix_free
              nonlocal t_yev, t_K4_free

              def f(t):
                  _w(t, t_bd_grp[grp], t_psI1_free)
                  for j in range(4):
                      ins = t.matmul(psI1[:, j, :], bdq[:, g0 + j, :],
                                     bdk[:, g0 + j, :], start=True, stop=True)
                  sPE.bump(ins)
              block.tensor(f)
              t_kl = sPE.tok()
              yield

              def f(v, t_kl=t_kl):
                  _w(v, t_kl)
                  v.tensor_reduce(kred[0:64, 0:4], psI1[0:64, :, 0:64], AX.X,
                                  ALU.max, negate=True)
                  sDV.bump(v.tensor_reduce(kred[64:128, 0:4],
                                           psI1[64:128, :, 64:128], AX.X,
                                           ALU.max, negate=True))
              vec(f)
              t_kmax = sDV.tok()
              yield

              def f(s, t_kmax=t_kmax, t_K4_free=t_K4_free):
                  _w(s, t_kmax, t_K4_free)
                  for j in range(4):
                      s.activation(K4[0:64, j, 0:64], psI1[0:64, j, 0:64],
                                   AF.Exp, bias=kred[0:64, j:j + 1],
                                   accum_out=kred[0:64, 4 + j:5 + j])
                      ins = s.activation(K4[64:128, j, 64:128],
                                         psI1[64:128, j, 64:128],
                                         AF.Exp, bias=kred[64:128, j:j + 1],
                                         accum_out=kred[64:128, 4 + j:5 + j])
                  sAC.bump(ins)
              act(f)
              t_kexp = sAC.tok()
              t_psI1_free = t_kexp
              yield

              def f(v, t_kexp=t_kexp):
                  _w(v, t_kexp)
                  v.reciprocal(kred[:, 4:8], kred[:, 4:8])
                  v.drain()
                  for j in range(4):
                      ins = v.tensor_scalar(K4[:, j, :], K4[:, j, :],
                                            kred[:, 4 + j:5 + j], None,
                                            ALU.mult)
                  sDV.bump(ins)
              vec(f)
              t_K = sDV.tok()
              yield

              def f(t, t_K=t_K):
                  _w(t, t_K, t_psRow_free)
                  sPE.bump(t.matmul(psRow[0:2, :], c_sb["onescol2f"][:],
                                    K4[:].rearrange("p a b -> p (a b)"),
                                    start=True, stop=True))
              block.tensor(f)
              t_cs = sPE.tok()
              yield

              def f(v, t_cs=t_cs):
                  _w(v, t_cs)
                  v.tensor_reduce(cden[:],
                                  psRow[0:2, :].rearrange(
                                      "p (a b) -> p a b", a=4),
                                  AX.X, ALU.max)
                  v.drain()
                  sDV.bump(v.reciprocal(cden[:], cden[:]))
              vec(f)
              t_cden = sDV.tok()
              t_psRow_free = t_cden
              yield

              def f(t, t_cden=t_cden, t_mf=t_mix_free):
                  _w(t, t_cden, t_mf)
                  sPE.bump(t.matmul(psSm[:, 0:4], c_sb["sel2f"][:], cden[:],
                                    start=True, stop=True))
              block.tensor(f)
              t_dnmm = sPE.tok()

              def f(v, t_dnmm=t_dnmm):
                  _w(v, t_dnmm)
                  sDV.bump(v.tensor_copy(denominv[:], psSm[:, 0:4]))
              vec(f)
              t_dn = sDV.tok()
              t_mix_free = t_dn
              yield

              def f(v, t_dn=t_dn, grp=grp):
                  _w(v, t_dn)
                  if grp == 1:
                      _w(v, t_pnv[0])
                  v.drain()
                  for j in range(4):
                      ins = v.tensor_scalar(TVg[:, j, :], K4[:, j, :],
                                            denominv[:, j:j + 1], None,
                                            ALU.mult)
                  sDV.bump(ins)
              vec(f)
              t_tv0 = sDV.tok()
              yield

              t_trev = t_tv0
              for j in range(4):
                  def f(t, j=j, t_trev=t_trev, t_mf=t_mix_free):
                      _w(t, t_K, t_trev, t_mf)
                      sPE.bump(t.transpose(psT128, K4[:, j, :],
                                           c_sb["eye128f"][:]))
                  block.tensor(f)
                  t_tr2 = sPE.tok()

                  def f(v, j=j, t_tr2=t_tr2):
                      _w(v, t_tr2)
                      v.tensor_copy(TK4[:, j, :], psT128)
                      sDV.bump(v.tensor_scalar(Vg[:, j, :], psT128,
                                               denominv[:, j:j + 1], None,
                                               ALU.mult))
                  vec(f)
                  t_trev = sDV.tok()
                  t_mix_free = t_trev
                  yield
              t_K4_free = t_trev
              t_state = t_trev

              for it in range(N_ITER):
                  last = it == N_ITER - 1

                  def f(t, t_state=t_state, tI1=t_psI1_free,
                        tI2=t_psI2_free):
                      _w(t, t_state, tI1, tI2)
                      for j in range(4):
                          t.matmul(psI1[:, j, :], TK4[:, j, :], Vg[:, j, :],
                                   start=True, stop=True)
                      for j in range(4):
                          ins = t.matmul(psI2[:, j, :], Vg[:, j, :],
                                         TK4[:, j, :], start=True, stop=True)
                      sPE.bump(ins)
                  block.tensor(f)
                  t_p1 = sPE.tok()

                  def f(v, t_p1=t_p1):
                      _w(v, t_p1)
                      v.scalar_tensor_tensor(
                          Tg[:], psI1[:], -1.0,
                          c_sb["eye7bd"][:].unsqueeze(1).to_broadcast(
                              [128, 4, 128]),
                          ALU.mult, ALU.add)
                      sDV.bump(v.tensor_copy(TP1g[:], psI2[:]))
                  vec(f)
                  t_T1 = sDV.tok()
                  t_psI1_free = t_T1
                  t_psI2_free = t_T1
                  yield

                  def f(t, t_T1=t_T1):
                      _w(t, t_T1)
                      for j in range(4):
                          ins = t.matmul(psI1[:, j, :], TP1g[:, j, :],
                                         Tg[:, j, :], start=True, stop=True)
                      sPE.bump(ins)
                  block.tensor(f)
                  t_p2 = sPE.tok()

                  def f(v, t_p2=t_p2):
                      _w(v, t_p2)
                      sDV.bump(v.scalar_tensor_tensor(
                          Tg[:], psI1[:], -1.0,
                          c_sb["eye15bd"][:].unsqueeze(1).to_broadcast(
                              [128, 4, 128]),
                          ALU.mult, ALU.add))
                  vec(f)
                  t_T2 = sDV.tok()
                  t_psI1_free = t_T2
                  yield

                  def f(t, t_T2=t_T2):
                      _w(t, t_T2)
                      for j in range(4):
                          ins = t.matmul(psI2[:, j, :], TP1g[:, j, :],
                                         Tg[:, j, :], start=True, stop=True)
                      sPE.bump(ins)
                  block.tensor(f)
                  t_p3 = sPE.tok()

                  def f(v, t_p3=t_p3):
                      _w(v, t_p3)
                      sDV.bump(v.scalar_tensor_tensor(
                          Tg[:], psI2[:], -1.0,
                          c_sb["eye13bd"][:].unsqueeze(1).to_broadcast(
                              [128, 4, 128]),
                          ALU.mult, ALU.add))
                  vec(f)
                  t_T3 = sDV.tok()
                  t_psI2_free = t_T3
                  yield

                  def f(t, t_T3=t_T3, last=last, tI1=t_psI1_free):
                      _w(t, t_T3, tI1)
                      if not last:
                          for j in range(4):
                              t.matmul(psI1[:, j, :], TVg[:, j, :],
                                       Tg[:, j, :], start=True, stop=True)
                      for j in range(4):
                          ins = t.matmul(psI2[:, j, :], Tg[:, j, :],
                                         TVg[:, j, :], start=True, stop=True)
                      sPE.bump(ins)
                  block.tensor(f)
                  t_vmm = sPE.tok()

                  def f(v, t_vmm=t_vmm, last=last):
                      _w(v, t_vmm)
                      if not last:
                          v.tensor_scalar(Vg[:], psI1[:], 0.25, None,
                                          ALU.mult)
                      sDV.bump(v.tensor_scalar(TVg[:], psI2[:], 0.25, None,
                                               ALU.mult))
                  vec(f)
                  t_state = sDV.tok()
                  t_psI1_free = t_state
                  t_psI2_free = t_state
                  yield

              while zun_gate is not None and not zun_gate[0]:
                  yield

              def f(t, t_state=t_state, tI1=t_psI1_free,
                    tE=t_psE_free_tok[0]):
                  _w(t, t_state, tI1, tE)
                  for j in range(4):
                      ins = t.matmul(psI1[:, j, :], TVg[:, j, :],
                                     zun[:, g0 + j, :], start=True, stop=True)
                  sPE.bump(ins)
              block.tensor(f)
              t_pnvmm = sPE.tok()

              def f(v, g0=g0, t_pnvmm=t_pnvmm):
                  _w(v, t_pnvmm)
                  v.tensor_copy(PNV[0:64, g0:g0 + 4, 0:64],
                                psI1[0:64, :, 0:64])
                  sDV.bump(v.tensor_copy(PNV[64:128, g0:g0 + 4, 64:128],
                                         psI1[64:128, :, 64:128]))
              vec(f)
              t_pnv[grp] = sDV.tok()
              t_psI1_free = t_pnv[grp]
              yield

              rotY = [psI1, psI2]
              for pp in range(g0, g0 + 4):
                  for g in range(4):
                      yb = g % 2
                      tfree = t_psI1_free if yb == 0 else t_psI2_free

                      def f(t, pp=pp, g=g, yb=yb, grp=grp, tfree=tfree):
                          _w(t, t_pnv[grp], t_ELs_done[pp], tfree)
                          out = rotY[yb][:].rearrange("q a b -> q (a b)")
                          sPE.bump(t.matmul(out, PNV[:, pp, :],
                                            ELs[:, pp,
                                                512 * g:512 * (g + 1)],
                                            start=True, stop=True))
                      block.tensor(f)
                      t_ymm = sPE.tok()

                      def f(v, pp=pp, g=g, yb=yb, t_ymm=t_ymm):
                          _w(v, t_ymm)
                          sDV.bump(v.tensor_copy(
                              yT[:, pp, 512 * g:512 * (g + 1)],
                              rotY[yb][:].rearrange("q a b -> q (a b)")))
                      vec(f)
                      if yb == 0:
                          t_psI1_free = sDV.tok()
                      else:
                          t_psI2_free = sDV.tok()
                      yield
              t_yev = sDV.tok()

            if grp == 0 and DEFER_P:
                pending_p[0] = emit_pphase()
            else:
                gd7 = pending_d7[0]
                pending_d7[0] = None
                gp1 = pending_p[0]
                pending_p[0] = None
                if gp1 is None:
                    gp1 = emit_pphase()
                live = [g for g in (gd7, gp1) if g is not None]
                while live:
                    for g in list(live):
                        if next(g, "END") == "END":
                            live.remove(g)

        tap("bdq", bdq[:], [128, NPAIR, 128], BF16, [t_yev])
        tap("bdk", bdk[:], [128, NPAIR, 128], BF16, [t_yev])
        tap("zun", zun[:], [128, NPAIR, 128], BF16, [t_yev])
        tap("PNV", PNV[:], [128, NPAIR, 128], BF16, [t_yev])
        tap("sinv", sinv[:], [128, NPAIR], F32, [t_yev])
        tap("ELs", ELs[:], [128, NPAIR, TOK], BF16, [t_yev])

        def f(g):
            _w(g, qS.tok(), qG.tok(), qC.tok(), qWs[0].tok(), qWs[1].tok())
        block.gpsimd(f)
        barrier()
        attn.__exit__(None, None, None)
        t_R4 = [None, None, None, None]

        ffn = contextlib.ExitStack()
        ffn.__enter__()
        psF = [ffn.enter_context(nc.psum_tensor(f"psF{i}", [128, 512], F32))
               for i in range(4)]
        bank8 = [psA, psB, psC, psD] + psF

        Wp_sb = ffn.enter_context(nc.sbuf_tensor("Wp_sb", [128, NDC, C], BF16))
        b2row = ffn.enter_context(nc.sbuf_tensor("b2row", [1, C], F32))
        b2b = ffn.enter_context(nc.sbuf_tensor("b2b", [128, C], F32))
        x0c = ffn.enter_context(nc.sbuf_tensor("x0c", [128, 4, C], F32))
        x1c = [ffn.enter_context(nc.sbuf_tensor(f"x1c{i}", [128, 4, C], F32))
               for i in range(2)]
        x1bc = ffn.enter_context(nc.sbuf_tensor("x1bc", [128, 4, C], BF16))
        x1Tc = [ffn.enter_context(
            nc.sbuf_tensor(f"x1Tc{i}", [128, 4, NDC, 128], BF16))
            for i in range(2)]
        hTc = ffn.enter_context(nc.sbuf_tensor("hTc", [128, NFC, CHT], BF16))
        w1s = [ffn.enter_context(nc.sbuf_tensor(f"w1s{i}", [128, NDC, 512],
                                                BF16)) for i in range(2)]
        w2s = [ffn.enter_context(nc.sbuf_tensor(f"w2s{i}", [128, C], BF16))
               for i in range(2)]
        r_sb = ffn.enter_context(nc.sbuf_tensor("r_sb", [8, 512], F32))
        rlog = ffn.enter_context(nc.sbuf_tensor("rlog", [128, 4, 8], F32))
        rex = ffn.enter_context(nc.sbuf_tensor("rex", [128, 4, 8], F32))
        rst = ffn.enter_context(nc.sbuf_tensor("rst", [128, 8], F32))
        p7c = ffn.enter_context(nc.sbuf_tensor("p7c", [128, 4], F32))
        stats = ffn.enter_context(nc.sbuf_tensor("stats", [128, 48], F32))
        mv = ffn.enter_context(nc.sbuf_tensor("mv", [128, 4, 2], F32))
        rstd = ffn.enter_context(nc.sbuf_tensor("rstd", [128, 4], F32))
        nmr = ffn.enter_context(nc.sbuf_tensor("nmr", [128, 4], F32))
        pre8 = ffn.enter_context(nc.sbuf_tensor("pre8", [128, 8, 512], F32))

        def f(s):
            qC.bump(s.dma_start(out=b2row[:], in_=b2.unsqueeze(0)), 16)
        act(f)
        t_b2row = qC.tok()

        def f(t):
            _w(t, t_b2row)
            t.matmul(psA[:], c_sb["ones1f"][:], b2row[0:1, 0:512],
                     start=True, stop=True)
            sPE.bump(t.matmul(psB[:], c_sb["ones1f"][:], b2row[0:1, 512:1024],
                              start=True, stop=True))
        block.tensor(f)
        t_b2mm = sPE.tok()

        def f(v):
            _w(v, t_b2mm)
            v.tensor_copy(b2b[:, 0:512], psA[:])
            sDV.bump(v.tensor_copy(b2b[:, 512:1024], psB[:]))
        vec(f)
        t_b2 = sDV.tok()

        def f(s):
            _w(s, t_scrp[0])
            qC.bump(s.dma_start(out=Wp_sb[:], in_=scr_Wp), 16)
        block.scalar(f)
        t_wp = qC.tok()

        t_x0 = [None]

        def load_x0(c, wait_tok):
            if c >= NCH:
                return

            def f(s, c=c, wait_tok=wait_tok):
                _w(s, wait_tok)
                t0 = CHT * c
                t_x0[0] = qX[0].bump(s.dma_start(
                    out=x0c[:],
                    in_=x_own[t0:t0 + CHT, :].rearrange(
                        "(tc tp) d -> tp tc d", tp=128)), 16)
            act(f)

        load_x0(0, None)

        t_w1 = [None, None]
        t_w1buf = [None, None]

        def load_w1(c, si):
            def f(sy, c=c, si=si):
                _w(sy, t_w1buf[si % 2], t_scr[0])
                t_w1[si % 2] = qF1[si % 2].bump(sy.dma_start(
                    out=w1s[si % 2][:], in_=scr_W1[si]), 16)
            block.sync(f)

        t_w2 = [None, None]
        t_w2buf = [None, None]

        def load_w2(c, f4):
            def f(s, c=c, f4=f4):
                _w(s, t_w2buf[f4 % 2], t_scr[0])
                t_w2[f4 % 2] = qF2[f4 % 2].bump(s.dma_start(
                    out=w2s[f4 % 2][:], in_=scr_W2[f4]), 16)
            block.scalar(f)

        t_banks_free = None
        t_store = [None, None]
        t_x1T_done = [None, None]
        t_x1bc_free = None
        t_ln_act_prev = None
        t_psRow2_free = None
        t_psSm2_free = None
        t_p7 = None

        pending_ln2 = [None]
        t_epi_sub = None
        t_pre_free = None

        for c in range(NCH):
            cb = c % 2

            load_w1(c, 0)
            load_w1(c, 1)
            load_w2(c, 0)
            load_w2(c, 1)

            t_wp_sub = [None] * 4

            def f(t, c=c, t_epi_sub=t_epi_sub):
                _w(t, t_wp, t_psRow2_free, t_psSm2_free)
                if c == 0:
                    _w(t, t_yev, t_b2)
                for sub in range(4):
                    if t_epi_sub is not None:
                        _w(t, t_epi_sub[sub])
                    tsl = slice(512 * c + 128 * sub,
                                512 * c + 128 * (sub + 1))
                    for dc in range(NDC):
                        for hh in range(2):
                            ins = t.matmul(
                                bank8[2 * sub + hh][:], yT[:, dc, tsl],
                                Wp_sb[:, dc, 512 * hh:512 * (hh + 1)],
                                start=(dc == 0), stop=(dc == NDC - 1))
                    t_wp_sub[sub] = sPE.bump(ins)
            block.tensor(f)

            def f(v, c=c, cb=cb, t_wp_sub=t_wp_sub):
                _w(v, t_x0[0], t_store[cb])
                for sub in range(4):
                    _w(v, t_wp_sub[sub])
                    for hh in range(2):
                        v.tensor_tensor(
                            x1c[cb][:, sub, 512 * hh:512 * (hh + 1)],
                            bank8[2 * sub + hh][:],
                            x0c[:, sub, 512 * hh:512 * (hh + 1)],
                            ALU.add)
                    v.drain()
                    v.bn_stats(stats[:, 12 * sub:12 * sub + 6],
                               x1c[cb][:, sub, 0:512].unsqueeze(1))
                    ins = v.bn_stats(stats[:, 12 * sub + 6:12 * sub + 12],
                                     x1c[cb][:, sub, 512:1024].unsqueeze(1))
                sDV.bump(ins)
            vec(f)
            t_x1pre = sDV.tok()
            t_banks_free = t_x1pre

            load_x0(c + 1, t_x1pre)

            def f(v, c=c, cb=cb, t_x1pre=t_x1pre):
                _w(v, t_x1pre, t_ln_act_prev)
                v.drain()
                for sub in range(4):
                    v.bn_aggr(mv[:, sub, :], stats[:, 12 * sub:12 * sub + 12])
                v.drain()
                sDV.bump(v.tensor_scalar(rstd[:], mv[:, :, 1], EPS, None,
                                         ALU.add))
            vec(f)
            t_bn = sDV.tok()

            def f(s, t_bn=t_bn):
                _w(s, t_bn)
                sAC.bump(s.activation(rstd[:], rstd[:], AF.Sqrt))
            act(f)
            t_sq = sAC.tok()

            def f(v, t_sq=t_sq):
                _w(v, t_sq)
                v.reciprocal(rstd[:], rstd[:])
                v.drain()
                sDV.bump(v.scalar_tensor_tensor(nmr[:], mv[:, :, 0], -1.0,
                                                rstd[:], ALU.mult, ALU.mult))
            vec(f)
            t_lns = sDV.tok()

            t_nbf = [None] * 4

            def f(v, c=c, cb=cb, t_lns=t_lns):
                _w(v, t_lns, t_x1bc_free)
                v.drain()
                for sub in range(4):
                    ins = v.tensor_scalar(x1bc[:, sub, :],
                                          x1c[cb][:, sub, :],
                                          rstd[:, sub:sub + 1],
                                          nmr[:, sub:sub + 1],
                                          ALU.mult, ALU.add)
                    t_nbf[sub] = sDV.bump(ins)
            vec(f)

            def f(sy, c=c, cb=cb, t_nbf=t_nbf):
                _w(sy, t_x1T_done[cb])
                for sub in range(4):
                    _w(sy, t_nbf[sub])
                    qS.bump(sy.dma_start(out=x1Tc[cb][:, sub, :, :],
                                         in_=x1bc[:, sub, :],
                                         transpose=True), 16)
            block.sync(f)
            t_x1T = qS.tok()
            t_x1bc_free = t_x1T

            def f(g, c=c, cb=cb, t_nbf3=t_nbf[3]):
                _w(g, t_nbf3)
                g.drain()
                for sub in range(4):
                    ins = g.tensor_scalar(x1c[cb][:, sub, :],
                                          x1c[cb][:, sub, :],
                                          rstd[:, sub:sub + 1],
                                          nmr[:, sub:sub + 1],
                                          ALU.mult, ALU.add)
                sPL.bump(ins)
            block.gpsimd(f)
            t_ln1f = sPL.tok()
            t_ln_act_prev = t_ln1f

            if pending_ln2[0] is not None:
                pending_ln2[0]()
                pending_ln2[0] = None

            def emit_router(c=c, cb=cb, t_x1T_tok=t_x1T):
                nonlocal t_psRow2_free, t_psSm2_free, t_p7
                def f(t, c=c, cb=cb):
                    _w(t, t_x1T_tok, t_banks_free)
                    for dc in range(NDC):
                        ins = t.matmul(psF[2][0:8, :], wr_sb[:, dc, :],
                                       x1Tc[cb][:, :, dc, :],
                                       start=(dc == 0), stop=(dc == NDC - 1))
                    sPE.bump(ins)
                block.tensor(f)
                t_rmm = sPE.tok()

                def f(v, t_rmm=t_rmm):
                    _w(v, t_rmm)
                    sDV.bump(v.tensor_copy(r_sb[:], psF[2][0:8, :]))
                vec(f)
                t_rev = sDV.tok()
                t_psRow2_free = t_rev

                t_rtr_ev = None
                for u in range(4):
                    def f(t, u=u, t_rev=t_rev, t_rtr_ev=t_rtr_ev):
                        _w(t, t_rev, t_rtr_ev)
                        sPE.bump(t.transpose(psF[3][:, 0:8],
                                             r_sb[:, 128 * u:128 * (u + 1)],
                                             c_sb["eye8f"][:]))
                    block.tensor(f)
                    t_rtr = sPE.tok()

                    def f(v, u=u, t_rtr=t_rtr):
                        _w(v, t_rtr)
                        sDV.bump(v.tensor_copy(rlog[:, u, :], psF[3][:, 0:8]))
                    vec(f)
                    t_rtr_ev = sDV.tok()
                t_psSm2_free = t_rtr_ev

                def f(v, t_rtr_ev=t_rtr_ev):
                    _w(v, t_rtr_ev)
                    v.drain()
                    v.tensor_reduce(rst[:, 0:4], rlog[:], AX.X, ALU.max)
                    v.drain()
                    sDV.bump(v.tensor_tensor(
                        rex[:], rlog[:],
                        rst[:, 0:4].unsqueeze(2).to_broadcast([128, 4, 8]),
                        ALU.subtract))
                vec(f)
                t_rsub = sDV.tok()

                def f(s, t_rsub=t_rsub):
                    _w(s, t_rsub)
                    sAC.bump(s.activation(rex[:], rex[:], AF.Exp))
                act(f)
                t_rexp = sAC.tok()

                def f(v, t_rexp=t_rexp):
                    _w(v, t_rexp)
                    v.tensor_reduce(rst[:, 4:8], rex[:], AX.X, ALU.add)
                    v.drain()
                    v.reciprocal(rst[:, 4:8], rst[:, 4:8])
                    v.drain()
                    sDV.bump(v.tensor_tensor(p7c[:], rex[:, :, 7], rst[:, 4:8],
                                             ALU.mult))
                vec(f)
                t_p7 = sDV.tok()

            rot6w = [psA, psB, psC, psD, psF[0], psF[1]]
            t_R6 = [t_banks_free] * 6
            mm4 = [None] * 4
            for si in range(8):
                if 1 <= si < 7:
                    load_w1(c, si + 1)
                if si == 1:
                    emit_router()
                for fc in range(4):
                    f4 = 4 * si + fc
                    bi = f4 % 6

                    def f(t, c=c, cb=cb, si=si, fc=fc, bi=bi,
                          t_x1T=t_x1T, t_R6b=t_R6[bi]):
                        _w(t, t_w1[si % 2])
                        if si == 0 and fc == 0:
                            _w(t, t_x1T)
                        _w(t, t_R6b)
                        ps = rot6w[bi]
                        for dc in range(NDC):
                            ins = t.matmul(
                                ps[:],
                                w1s[si % 2][:, dc, 128 * fc:128 * (fc + 1)],
                                x1Tc[cb][:, :, dc, :],
                                start=(dc == 0), stop=(dc == NDC - 1))
                        sPE.bump(ins)
                    block.tensor(f)
                    mm4[fc] = sPE.tok()

                    if fc == 3:
                        def f(s, f4=f4, mm4tok=mm4[3]):
                            _w(s, mm4tok)
                            for k_ in range(4):
                                fx = f4 - 3 + k_
                                ins = s.activation(
                                    hTc[:, fx, :], rot6w[fx % 6][:],
                                    AF.Gelu,
                                    bias=b1_sb[:, fx:fx + 1])
                                t_R6[fx % 6] = sAC.bump(ins)
                        act(f)
                t_w1buf[si % 2] = sPE.tok()
            t_x1T_done[cb] = sPE.tok()
            t_gelu = sAC.tok()

            t_w2bank = [None] * 8
            for f4 in range(NFC):
                if 1 <= f4 < NFC - 1:
                    load_w2(c, f4 + 1)

                def f(t, c=c, f4=f4):
                    _w(t, t_w2[f4 % 2], t_gelu)
                    if f4 == 0:
                        _w(t, t_psRow2_free, t_psSm2_free)
                    for sub in range(4):
                        for hh in range(2):
                            ins = t.matmul(
                                bank8[2 * sub + hh][:],
                                hTc[:, f4, 128 * sub:128 * (sub + 1)],
                                w2s[f4 % 2][:, 512 * hh:512 * (hh + 1)],
                                start=(f4 == 0), stop=(f4 == NFC - 1))
                            if f4 == NFC - 1:
                                t_w2bank[2 * sub + hh] = sPE.bump(ins)
                    if f4 != NFC - 1:
                        sPE.bump(ins)
                block.tensor(f)
                t_w2buf[f4 % 2] = sPE.tok()
            t_w2_done = sPE.tok()

            t_pre = [None] * 8

            def f(g, c=c, cb=cb, t_p7=t_p7, t_pre_free=t_pre_free):
                _w(g, t_p7, t_b2, t_pre_free)
                g.drain()
                for k in range(8):
                    sub, hh = divmod(k, 2)
                    g.tensor_scalar(pre8[:, k, :],
                                    b2b[:, 512 * hh:512 * (hh + 1)],
                                    p7c[:, sub:sub + 1], None, ALU.mult)
                g.drain()
                for k in range(8):
                    sub, hh = divmod(k, 2)
                    ins = g.tensor_tensor(
                        pre8[:, k, :], pre8[:, k, :],
                        x1c[cb][:, sub, 512 * hh:512 * (hh + 1)], ALU.add)
                    t_pre[k] = sPL.bump(ins)
            block.gpsimd(f)

            t_epi_sub = [None] * 4

            def f(v, c=c, cb=cb, t_pre=t_pre, t_w2bank=t_w2bank,
                  t_epi_sub=t_epi_sub, t_p7=t_p7):
                _w(v, t_p7)
                v.drain()
                for k in range(8):
                    sub, hh = divmod(k, 2)
                    _w(v, t_w2bank[k], t_pre[k])
                    ins = v.scalar_tensor_tensor(
                        x1c[cb][:, sub, 512 * hh:512 * (hh + 1)],
                        bank8[k][:], p7c[:, sub:sub + 1],
                        pre8[:, k, :], ALU.mult, ALU.add)
                    if k % 2 == 1:
                        t_epi_sub[sub] = sDV.bump(ins)
            vec(f)
            t_ffn_done = sDV.tok()
            t_banks_free = t_ffn_done
            t_pre_free = t_ffn_done

            def emit_ln2(c=c, cb=cb, t_ffn_done=t_ffn_done):
                nonlocal t_ln_act_prev
                def f(v, c=c, cb=cb):
                  _w(v, t_ffn_done, t_ln_act_prev)
                  v.drain()
                  for sub in range(4):
                      v.bn_stats(stats[:, 12 * sub:12 * sub + 6],
                                 x1c[cb][:, sub, 0:512].unsqueeze(1))
                      v.bn_stats(stats[:, 12 * sub + 6:12 * sub + 12],
                                 x1c[cb][:, sub, 512:1024].unsqueeze(1))
                  v.drain()
                  for sub in range(4):
                      v.bn_aggr(mv[:, sub, :], stats[:, 12 * sub:12 * sub + 12])
                  v.drain()
                  sDV.bump(v.tensor_scalar(rstd[:], mv[:, :, 1], EPS, None,
                                           ALU.add))
                vec(f)
                t_bn2 = sDV.tok()

                def f(s, t_bn2=t_bn2):
                  _w(s, t_bn2)
                  sAC.bump(s.activation(rstd[:], rstd[:], AF.Sqrt))
                act(f)
                t_sq2 = sAC.tok()

                def f(v, t_sq2=t_sq2):
                  _w(v, t_sq2)
                  v.reciprocal(rstd[:], rstd[:])
                  v.drain()
                  sDV.bump(v.scalar_tensor_tensor(nmr[:], mv[:, :, 0], -1.0,
                                                  rstd[:], ALU.mult, ALU.mult))
                vec(f)
                t_lns2 = sDV.tok()

                def f(g, c=c, cb=cb, t_lns2=t_lns2):
                  _w(g, t_lns2)
                  g.drain()
                  for sub in range(4):
                      ins = g.tensor_scalar(x1c[cb][:, sub, :],
                                            x1c[cb][:, sub, :],
                                            rstd[:, sub:sub + 1],
                                            nmr[:, sub:sub + 1],
                                            ALU.mult, ALU.add)
                  sPL.bump(ins)
                block.gpsimd(f)
                t_ln2 = sPL.tok()
                t_ln_act_prev = t_ln2

                def f(s, c=c, cb=cb, t_ln2=t_ln2):
                  _w(s, t_ln2)
                  t0 = CHT * c
                  t_store[cb] = qST.bump(s.dma_start(
                      out=out_h[t0:t0 + CHT, :].rearrange(
                          "(tc tp) d -> tp tc d", tp=128),
                      in_=x1c[cb][:]), 16)
                act(f)

            pending_ln2[0] = emit_ln2

        if pending_ln2[0] is not None:
            pending_ln2[0]()
            pending_ln2[0] = None

        def f(s):
            _w(s, qST.tok(), qC.tok(), qX[0].tok(), qX[1].tok())
        act(f)

        def f(g):
            _w(g, qG.tok(), qF1[0].tok(), qF1[1].tok(), qF2[0].tok(),
               qF2[1].tok(), qV.tok())
        block.gpsimd(f)

        ffn.__exit__(None, None, None)
    return nc


_CACHE = {}


def kernel(L0_attn_x, L0_attn_W_attn, L0_attn_W_proj, L0_ffn_Wr, L0_ffn_W1,
           L0_ffn_b1, L0_ffn_W2, L0_ffn_b2):
    x = np.ascontiguousarray(np.asarray(L0_attn_x, dtype=np.float32))
    W_attn = np.ascontiguousarray(np.asarray(L0_attn_W_attn, dtype=np.float32))
    W_proj = np.ascontiguousarray(np.asarray(L0_attn_W_proj, dtype=np.float32))
    WrT = np.ascontiguousarray(np.asarray(L0_ffn_Wr, dtype=np.float32).T)
    W1 = np.ascontiguousarray(np.asarray(L0_ffn_W1, dtype=np.float32)[NEXP - 1])
    b1v = np.ascontiguousarray(np.asarray(L0_ffn_b1, dtype=np.float32)[NEXP - 1])
    W2 = np.ascontiguousarray(np.asarray(L0_ffn_W2, dtype=np.float32)[NEXP - 1])
    b2v = np.ascontiguousarray(np.asarray(L0_ffn_b2, dtype=np.float32)[NEXP - 1])

    if "nc" not in _CACHE:
        _CACHE["nc"] = build_nc()
    nc = _CACHE["nc"]

    in_maps = []
    for core in range(8):
        b, h = divmod(core, 2)
        m = {
            "x_own": np.ascontiguousarray(x[b, h * TOK:(h + 1) * TOK]),
            "x_par": np.ascontiguousarray(x[b, (1 - h) * TOK:(2 - h) * TOK]),
            "W_attn": W_attn, "W_proj": W_proj, "WrT": WrT,
            "W1": W1, "b1": np.ascontiguousarray(b1v.reshape(NFC, 128).T),
            "W2": W2, "b2": b2v,
        }
        m.update(_host_consts(h))
        in_maps.append(m)

    res = run_bass_kernel_spmd(nc, in_maps, core_ids=list(range(8)))
    out = np.concatenate([res.results[i]["out"] for i in range(8)], axis=0)
    return out.reshape(B, T, C)


# version 22
# speedup vs baseline: 1.1629x; 1.1629x over previous
import contextlib

import numpy as np
import ml_dtypes

import concourse.bass as bass
import concourse.mybir as mybir
from concourse.bass_utils import run_bass_kernel_spmd

F32 = mybir.dt.float32
BF16 = mybir.dt.bfloat16
AF = mybir.ActivationFunctionType
ALU = mybir.AluOpType
AX = mybir.AxisListType

B, T, C = 4, 4096, 1024
NH, NL, HS = 16, 64, 64
FF = 4096
NEXP = 8
TOK = 2048
NTC = TOK // 128
NSC = 2 * NTC
NDC = C // 128
NPAIR = NH // 2
NFC = FF // 128
NCH = 4
CHT = TOK // NCH
N_ITER = 6
EPS = 1e-5

BF = ml_dtypes.bfloat16


def _host_consts(h):
    c = {}
    A = np.zeros((128, NSC, NL), np.float32)
    for i in range(NSC):
        half = h if i < NTC else 1 - h
        base = 32 * half + 2 * (i % NTC)
        A[0:64, i, base] = 1.0 / 64.0
        A[64:128, i, base + 1] = 1.0 / 64.0
    c["A_all"] = np.ascontiguousarray(A.astype(BF))
    c["eye64b"] = np.eye(64, dtype=BF)
    c["eye8f"] = np.eye(8, dtype=np.float32)
    eyebd = np.zeros((128, 128), np.float32)
    eyebd[0:64, 0:64] = np.eye(64)
    eyebd[64:128, 64:128] = np.eye(64)
    c["eye7bd"] = 7.0 * eyebd
    c["eye15bd"] = 15.0 * eyebd
    c["eye13bd"] = 13.0 * eyebd
    c["eye128f"] = np.eye(128, dtype=np.float32)
    oc2 = np.zeros((128, 2), np.float32)
    oc2[0:64, 0] = 1.0
    oc2[64:128, 1] = 1.0
    c["onescol2b"] = oc2.astype(BF)
    c["onescol2f"] = oc2
    sel = np.zeros((2, 128), np.float32)
    sel[0, 0:64] = 1.0
    sel[1, 64:128] = 1.0
    c["sel2f"] = sel
    c["ones1f"] = np.ones((1, 128), np.float32)
    return c


_CONST_SPECS = [
    ("eye64b", [64, 64], BF16),
    ("eye8f", [8, 8], F32),
    ("eye7bd", [128, 128], F32),
    ("eye15bd", [128, 128], F32),
    ("eye13bd", [128, 128], F32),
    ("eye128f", [128, 128], F32),
    ("onescol2b", [128, 2], BF16),
    ("onescol2f", [128, 2], F32),
    ("sel2f", [2, 128], F32),
    ("ones1f", [1, 128], F32),
]


class Ctr:

    def __init__(self, sem):
        self.sem = sem
        self.n = 0

    def bump(self, instr, k=1):
        instr.then_inc(self.sem, k)
        self.n += k
        return (self.sem, self.n)

    def tok(self):
        return (self.sem, self.n)


def _w(eng, *toks):
    for tok in toks:
        if tok is not None and tok[1] > 0:
            eng.wait_ge(tok[0], tok[1])


DEFER_P = True
DEFER_D3 = True


def build_nc(taps=False):
    nc = bass.Bass("TRN2", target_bir_lowering=False, debug=False,
                   detect_race_conditions=False)

    x_own = nc.dram_tensor("x_own", [TOK, C], F32, kind="ExternalInput").ap()
    x_par = nc.dram_tensor("x_par", [TOK, C], F32, kind="ExternalInput").ap()
    W_attn = nc.dram_tensor("W_attn", [C, 3 * C], F32, kind="ExternalInput").ap()
    W_proj = nc.dram_tensor("W_proj", [C, C], F32, kind="ExternalInput").ap()
    WrT = nc.dram_tensor("WrT", [C, NEXP], F32, kind="ExternalInput").ap()
    W1 = nc.dram_tensor("W1", [C, FF], F32, kind="ExternalInput").ap()
    b1 = nc.dram_tensor("b1", [128, NFC], F32, kind="ExternalInput").ap()
    W2 = nc.dram_tensor("W2", [FF, C], F32, kind="ExternalInput").ap()
    b2 = nc.dram_tensor("b2", [C], F32, kind="ExternalInput").ap()
    out_h = nc.dram_tensor("out", [TOK, C], F32, kind="ExternalOutput").ap()
    scr_W1 = nc.dram_tensor("scr_W1", [8, 128, NDC, 512], BF16,
                            kind="Internal").ap()
    scr_Wp = nc.dram_tensor("scr_Wp", [128, NDC, 1024], BF16,
                            kind="Internal").ap()
    scr_W2 = nc.dram_tensor("scr_W2", [NFC, 128, C], BF16,
                            kind="Internal").ap()
    consts = {n: nc.dram_tensor(n, s, d, kind="ExternalInput").ap()
              for n, s, d in _CONST_SPECS}
    A_h = nc.dram_tensor("A_all", [128, NSC, NL], BF16,
                         kind="ExternalInput").ap()

    es = contextlib.ExitStack()
    with es:
        qG = Ctr(es.enter_context(nc.semaphore("qG")))
        qS = Ctr(es.enter_context(nc.semaphore("qS")))
        qC = Ctr(es.enter_context(nc.semaphore("qC")))
        qWs = [Ctr(es.enter_context(nc.semaphore(f"qW{i}"))) for i in range(2)]
        qF1 = [Ctr(es.enter_context(nc.semaphore(f"qF1{i}"))) for i in range(2)]
        qF2 = [Ctr(es.enter_context(nc.semaphore(f"qF2{i}"))) for i in range(2)]
        qX = [Ctr(es.enter_context(nc.semaphore(f"qX{i}"))) for i in range(2)]
        qST = Ctr(es.enter_context(nc.semaphore("qST")))
        qV = Ctr(es.enter_context(nc.semaphore("qV")))
        sPE = Ctr(es.enter_context(nc.semaphore("sPE")))
        sDV = Ctr(es.enter_context(nc.semaphore("sDV")))
        sAC = Ctr(es.enter_context(nc.semaphore("sAC")))
        sPL = Ctr(es.enter_context(nc.semaphore("sPL")))
        sEX = Ctr(es.enter_context(nc.semaphore("sEX")))

        psA = es.enter_context(nc.psum_tensor("psA", [128, 512], F32))
        psB = es.enter_context(nc.psum_tensor("psB", [128, 512], F32))
        psC = es.enter_context(nc.psum_tensor("psC", [128, 512], F32))
        psD = es.enter_context(nc.psum_tensor("psD", [128, 512], F32))

        c_sb = {}
        for name, shape, dt in _CONST_SPECS:
            c_sb[name] = es.enter_context(nc.sbuf_tensor("c_" + name, shape, dt))
        yT = es.enter_context(nc.sbuf_tensor("yT", [128, NPAIR, TOK], BF16))
        xlF = es.enter_context(nc.sbuf_tensor("xlF", [128, NDC, NL], BF16))
        wr_sb = es.enter_context(nc.sbuf_tensor("wr_sb", [128, NDC, NEXP], BF16))
        b1_sb = es.enter_context(nc.sbuf_tensor("b1_sb", [128, NFC], F32))

        block = es.enter_context(nc.Block())

        def vec(f):
            block.vector(f)

        def act(f):
            block.scalar(f)
        bar = Ctr(es.enter_context(nc.semaphore("bar")))
        tap_list = []

        def tap(name, sb_ap, shape, dtype, wait_toks):
            if not taps:
                return
            hd = nc.dram_tensor("dbg_" + name, shape, dtype,
                                kind="ExternalOutput").ap()

            def ftap(s, hd=hd, sb_ap=sb_ap, wait_toks=wait_toks):
                _w(s, *wait_toks)
                qC.bump(s.dma_start(out=hd, in_=sb_ap), 16)
            act(ftap)
            tok = qC.tok()

            def fhold(e, tok=tok):
                _w(e, tok)
            vec(fhold)
            act(fhold)
            block.tensor(fhold)
            tap_list.append(name)

        def barrier():
            target = bar.n + 5

            def fb(e):
                if hasattr(e, "drain"):
                    bar.bump(e.drain())
                else:
                    bar.bump(e.nop())
                e.wait_ge(bar.sem, target)
            block.tensor(fb)
            block.vector(fb)
            block.scalar(fb)

            def fq(e):
                bar.bump(e.nop())
                e.wait_ge(bar.sem, target)
            block.sync(fq)
            block.gpsimd(fq)

        def f(s):
            for name, _, _ in _CONST_SPECS:
                qC.bump(s.dma_start(out=c_sb[name][:], in_=consts[name][:]), 16)
            qC.bump(s.dma_start(out=b1_sb[:], in_=b1[:]), 16)
        act(f)
        t_consts = qC.tok()

        attn = contextlib.ExitStack()
        attn.__enter__()
        xT = attn.enter_context(nc.sbuf_tensor("xT", [128, NSC, NDC, 128], BF16))

        early = contextlib.ExitStack()
        early.__enter__()
        xall = early.enter_context(nc.sbuf_tensor("xall", [128, NSC, C], BF16))
        xl_sb = early.enter_context(nc.sbuf_tensor("xl_sb", [64, C], BF16))
        psXL = early.enter_context(nc.psum_tensor("psXL", [64, C], F32))
        psTbE = early.enter_context(nc.psum_tensor("psTbE", [128, 64], BF16))
        A_sb = early.enter_context(nc.sbuf_tensor("A_sb", [128, NSC, NL], BF16))

        def f(s):
            qC.bump(s.dma_start(out=A_sb[:], in_=A_h[:]), 16)
        act(f)
        t_A = qC.tok()

        def f(g):
            qG.bump(g.dma_start(
                out=xall[:, 0:NTC, :],
                in_=x_own.rearrange("(tc tp) d -> tp tc d", tp=128)), 16)
        block.gpsimd(f)
        t_xown = qG.tok()

        def f(g):
            qG.bump(g.dma_start(
                out=xall[:, NTC:NSC, :],
                in_=x_par.rearrange("(tc tp) d -> tp tc d", tp=128)), 16)
            qG.bump(g.dma_start(
                out=wr_sb[:], in_=WrT.rearrange("(c p) e -> p c e", p=128)), 16)
        block.gpsimd(f)
        t_xall = (qG.sem, 48)

        def f(sy):
            _w(sy, t_xown)
            qS.bump(sy.dma_start(out=xT[:, 0:NTC, :, :],
                                 in_=xall[:, 0:NTC, :], transpose=True), 16)
        block.sync(f)
        t_xT_own = qS.tok()

        def f(sy):
            _w(sy, t_xall)
            qS.bump(sy.dma_start(out=xT[:, NTC:NSC, :, :],
                                 in_=xall[:, NTC:NSC, :], transpose=True), 16)
        block.sync(f)
        t_xT = qS.tok()

        def f(t):
            _w(t, t_xall, t_consts, t_A)
            for half in range(2):
                for i in range(NSC):
                    ins = t.matmul(psXL[:, 512 * half:512 * (half + 1)],
                                   A_sb[:, i, :],
                                   xall[:, i, 512 * half:512 * (half + 1)],
                                   start=(i == 0), stop=(i == NSC - 1))
                sPE.bump(ins)
        block.tensor(f)
        t_xlmm = sPE.tok()

        def f(v):
            _w(v, t_xlmm)
            sDV.bump(v.tensor_copy(xl_sb[:], psXL[:]))
        vec(f)
        t_xl = sDV.tok()

        t_tr_ev = None
        for dc in range(NDC):
            def f(t, dc=dc, t_tr_ev=t_tr_ev):
                _w(t, t_xl, t_tr_ev)
                sPE.bump(t.transpose(psTbE[:],
                                     xl_sb[:, 128 * dc:128 * (dc + 1)],
                                     c_sb["eye64b"][:]))
            block.tensor(f)
            t_tr = sPE.tok()

            def f(v, dc=dc, t_tr=t_tr):
                _w(v, t_tr)
                sDV.bump(v.tensor_copy(xlF[:, dc, :], psTbE[:]))
            vec(f)
            t_tr_ev = sDV.tok()
        t_xlF = sDV.tok()

        def f(g):
            _w(g, t_xT)
        block.gpsimd(f)
        barrier()
        early.__exit__(None, None, None)

        psI1 = attn.enter_context(nc.psum_tensor("psI1", [128, 4, 128], F32))
        psI2 = attn.enter_context(nc.psum_tensor("psI2", [128, 4, 128], F32))
        psRow = attn.enter_context(nc.psum_tensor("psRow", [8, 512], F32))
        psMisc = attn.enter_context(nc.psum_tensor("psMisc", [128, 512], F32))
        psE = psMisc[:, 0:128]
        psSm = psMisc[:, 128:192]
        psT128 = psMisc[:, 192:320]
        wq = [attn.enter_context(nc.sbuf_tensor(f"wq{i}", [128, NDC, 128], BF16))
              for i in range(2)]
        wk = [attn.enter_context(nc.sbuf_tensor(f"wk{i}", [128, NDC, 128], BF16))
              for i in range(2)]
        wv = [attn.enter_context(nc.sbuf_tensor(f"wv{i}", [128, NDC, 128], BF16))
              for i in range(2)]
        qTp = attn.enter_context(nc.sbuf_tensor("qTp", [128, TOK], BF16))
        kTp = attn.enter_context(nc.sbuf_tensor("kTp", [128, T], BF16))
        vTp = attn.enter_context(nc.sbuf_tensor("vTp", [128, T], BF16))
        vtok = attn.enter_context(nc.sbuf_tensor("vtok", [128, NSC, 128], BF16))
        Elm = attn.enter_context(nc.sbuf_tensor("Elm", [128, T], BF16))
        Etok = attn.enter_context(nc.sbuf_tensor("Etok", [128, NSC, 128], BF16))
        ELs = attn.enter_context(nc.sbuf_tensor("ELs", [128, NPAIR, TOK], BF16))
        bdq = attn.enter_context(nc.sbuf_tensor("bdq", [128, NPAIR, 128], BF16))
        bdk = attn.enter_context(nc.sbuf_tensor("bdk", [128, NPAIR, 128], BF16))
        zun = attn.enter_context(nc.sbuf_tensor("zun", [128, NPAIR, 128], BF16))
        ssum8 = attn.enter_context(nc.sbuf_tensor("ssum8", [128, 8], F32))
        sinv = attn.enter_context(nc.sbuf_tensor("sinv", [128, NPAIR], F32))
        d_sb = attn.enter_context(nc.sbuf_tensor("d_sb", [2, 256], F32))
        dinv = attn.enter_context(nc.sbuf_tensor("dinv", [2, 256], F32))
        K4 = attn.enter_context(nc.sbuf_tensor("K4", [128, 4, 128], F32))
        TK4 = attn.enter_context(nc.sbuf_tensor("TK4", [128, 4, 128], BF16))
        kred = attn.enter_context(nc.sbuf_tensor("kred", [128, 8], F32))
        cden = attn.enter_context(nc.sbuf_tensor("cden", [2, 4], F32))
        denominv = attn.enter_context(nc.sbuf_tensor("denominv", [128, 4], F32))
        Vg = attn.enter_context(nc.sbuf_tensor("Vg", [128, 4, 128], BF16))
        TVg = attn.enter_context(nc.sbuf_tensor("TVg", [128, 4, 128], BF16))
        Tg = attn.enter_context(nc.sbuf_tensor("Tg", [128, 4, 128], BF16))
        TP1g = attn.enter_context(nc.sbuf_tensor("TP1g", [128, 4, 128], BF16))
        PNV = attn.enter_context(nc.sbuf_tensor("PNV", [128, NPAIR, 128], BF16))

        def f(v):
            v.memset(bdq[:], 0.0)
            v.memset(bdk[:], 0.0)
            v.memset(PNV[:], 0.0)
            v.memset(K4[:], 0.0)
            sDV.bump(v.drain())
        vec(f)
        t_bdz = sDV.tok()

        rot4 = [psA, psB, psC, psD]
        t_R4 = [None, None, None, None]
        rr = [0]

        def nextbank():
            bi = rr[0] % 4
            rr[0] += 1
            return bi
        t_mix_free = None
        t_psE_free_tok = [None]
        t_psRow_free = None
        t_psI1_free = None
        t_psI2_free = None
        t_Etok_tr_prev = None
        t_vtr_prev = None
        t_zu_prev = None
        t_stripe = [None] * NPAIR
        t_stripe_free = [None] * NPAIR
        t_ELs_done = [None] * NPAIR
        t_sinv = [None] * NPAIR
        t_bd_grp = [None, None]
        t_pnv = [None, None]
        t_yev = None

        def load_stripes(p):
            def f(g, p=p):
                if p >= 2:
                    _w(g, t_stripe_free[p - 2])
                buf = p % 2
                for w_sb, col0 in ((wq, 128 * p), (wk, C + 128 * p),
                                   (wv, 2 * C + 128 * p)):
                    src = W_attn[:, col0:col0 + 128]
                    t_stripe[p] = qWs[buf].bump(g.dma_start(
                        out=w_sb[buf][:],
                        in_=src.rearrange("(dc dp) f -> dp dc f", dp=128)), 16)
            block.gpsimd(f)

        load_stripes(0)
        load_stripes(1)

        dbg1 = attn.enter_context(nc.sbuf_tensor("dbg1", [128, 512], F32)) if taps else None

        def proj_block(p, w_sb, n_groups, dst, with_lm, extra_wait, evac,
                       hook=None, evac_toks=None):
            buf = p % 2
            ctr = sDV if evac == "v" else sAC
            for g in range(n_groups):
                if hook is not None:
                    hook()
                bi = nextbank()

                def f(t, g=g, bi=bi):
                    _w(t, t_R4[bi])
                    if g == 0:
                        _w(t, t_stripe[p], t_xT_own, t_xlF, *extra_wait)
                    if g == 4:
                        _w(t, t_xT)
                    ps = rot4[bi]
                    for dc in range(NDC):
                        ins = t.matmul(ps[:], w_sb[buf][:, dc, :],
                                       xT[:, 4 * g:4 * g + 4, dc, :],
                                       start=(dc == 0), stop=(dc == NDC - 1))
                    sPE.bump(ins)
                block.tensor(f)
                mm_tok = sPE.tok()

                def fe(e, g=g, bi=bi, mm_tok=mm_tok):
                    _w(e, mm_tok)
                    if evac == "v":
                        ins = e.tensor_copy(dst[:, 512 * g:512 * (g + 1)],
                                            rot4[bi][:])
                    else:
                        ins = e.copy(dst[:, 512 * g:512 * (g + 1)],
                                     rot4[bi][:])
                    t_R4[bi] = ctr.bump(ins)
                    if evac_toks is not None:
                        evac_toks.append(ctr.tok())
                (vec if evac == "v" else act)(fe)
            if with_lm:
                def f(t):
                    _w(t, t_mix_free)
                    for dc in range(NDC):
                        ins = t.matmul(psSm, w_sb[buf][:, dc, :],
                                       xlF[:, dc, :],
                                       start=(dc == 0), stop=(dc == NDC - 1))
                    sPE.bump(ins)
                block.tensor(f)
            return sPE.tok(), ctr.tok()

        t_minv_mm = None
        pending_d = [None]
        pending_p = [None]
        pending_d7 = [None]
        zgate1 = [True]
        t_K4_free = None

        t_scr = [None]

        t_scrp = [None]

        for p in range(NPAIR):
            if p == 1:
                def f(g):
                    t_scrp[0] = qG.bump(g.dma_start(
                        out=scr_Wp,
                        in_=W_proj.rearrange("(dc dp) f -> dp dc f",
                                             dp=128)), 16)
                    for si in range(8):
                        qG.bump(g.dma_start(
                            out=scr_W1[si],
                            in_=W1[:, 512 * si:512 * (si + 1)].rearrange(
                                "(dc dp) f -> dp dc f", dp=128)), 16)
                    t_scr[0] = qG.bump(g.dma_start(
                        out=scr_W2,
                        in_=W2.rearrange("(st dp) f -> st dp f", dp=128)), 16)
                block.gpsimd(f)

            gen_d = pending_d[0]
            pending_d[0] = None

            def hook_d(gen_d=gen_d):
                if gen_d is not None and next(gen_d, "END") != "END":
                    return
                if pending_p[0] is not None:
                    next(pending_p[0], None)

            t_qlmm, t_qevac = proj_block(p, wq, 4, qTp, True, [], "v",
                                         hook=hook_d)

            def f(v, p=p, t_qlmm=t_qlmm):
                _w(v, t_qlmm, t_bdz)
                v.tensor_copy(bdq[0:64, p, 0:64], psSm[0:64])
                sDV.bump(v.tensor_copy(bdq[64:128, p, 64:128], psSm[64:128]))
            vec(f)
            t_mix_free = sDV.tok()

            t_klmm, t_kevac = proj_block(p, wk, 8, kTp, True, [], "v",
                                         hook=hook_d)

            def f(v, p=p, t_klmm=t_klmm):
                _w(v, t_klmm)
                v.tensor_copy(bdk[0:64, p, 0:64], psSm[0:64])
                sDV.bump(v.tensor_copy(bdk[64:128, p, 64:128], psSm[64:128]))
            vec(f)
            t_bdk_p = sDV.tok()
            t_mix_free = t_bdk_p
            if p in (3, 7):
                t_bd_grp[p // 4] = t_bdk_p
            if p == 7:
                if pending_p[0] is not None:
                    for _ in pending_p[0]:
                        pass
                zgate1[0] = False
                pending_p[0] = emit_pphase(grp=1, g0=4, zun_gate=zgate1)

            vevac_toks = []
            _, t_vevac = proj_block(
                p, wv, 8, vTp, False,
                [t_vtr_prev] if t_vtr_prev else [], "v", hook=hook_d,
                evac_toks=vevac_toks)
            if gen_d is not None:
                for _ in gen_d:
                    pass
            t_stripe_free[p] = sPE.tok()
            if p + 2 < NPAIR:
                load_stripes(p + 2)

            vtok_toks = []

            def f(sy, p=p, t_vevac=t_vevac, vevac_toks=list(vevac_toks)):
                _w(sy, t_zu_prev)
                _w(sy, vevac_toks[3])
                qS.bump(sy.dma_start(out=vtok[:, 0:16, :],
                                     in_=vTp[:, 0:2048], transpose=True), 16)
                vtok_toks.append(qS.tok())
                _w(sy, t_vevac)
                qS.bump(sy.dma_start(out=vtok[:, 16:32, :],
                                     in_=vTp[:, 2048:4096], transpose=True),
                        16)
                vtok_toks.append(qS.tok())
            block.sync(f)
            t_vtok_tr = qS.tok()
            t_vtr_prev = t_vtok_tr
            vtok_toks = [vtok_toks[0]] * 4 + [vtok_toks[1]] * 4

            psI1f = psI1[:].rearrange("p a b -> p (a b)")
            psI2f = psI2[:].rearrange("p a b -> p (a b)")
            use6 = p < 4
            nbase = sEX.n
            nmm_toks = []
            nbanks = []
            for g in range(8):
                if use6 and g == 4:
                    bnk, told = psI1f, t_psI1_free
                    t_psI1_free = (sEX.sem, nbase + g + 1)
                elif use6 and g == 5:
                    bnk, told = psI2f, t_psI2_free
                    t_psI2_free = (sEX.sem, nbase + g + 1)
                else:
                    bi = nextbank()
                    bnk, told = rot4[bi][:], t_R4[bi]
                    t_R4[bi] = (sEX.sem, nbase + g + 1)
                nbanks.append(bnk)

                def f(t, p=p, g=g, bnk=bnk, told=told,
                      t_kevac=t_kevac, t_bdk_p=t_bdk_p):
                    _w(t, told)
                    if g == 0:
                        _w(t, t_kevac, t_bdk_p)
                    sPE.bump(t.matmul(bnk, bdq[:, p, :],
                                      kTp[:, 512 * g:512 * (g + 1)],
                                      start=True, stop=True))
                block.tensor(f)
                nmm_toks.append(sPE.tok())
                hook_d()
            t_Nmm = sPE.tok()

            def f(s, p=p, nmm_toks=list(nmm_toks),
                  t_Etok_tr_prev=t_Etok_tr_prev, nbanks=list(nbanks)):
                for g in range(8):
                    _w(s, nmm_toks[g])
                    if g == 0:
                        _w(s, t_Etok_tr_prev)
                    sEX.bump(s.activation(
                        Elm[:, 512 * g:512 * (g + 1)], nbanks[g],
                        AF.Exp, accum_out=ssum8[:, g:g + 1]))
            act(f)
            assert sEX.n == nbase + 8
            t_nexp = sEX.tok()

            def f(v, p=p, t_nexp=t_nexp):
                _w(v, t_nexp)
                v.tensor_reduce(sinv[:, p:p + 1], ssum8[:], AX.X, ALU.add)
                v.drain()
                sDV.bump(v.reciprocal(sinv[:, p:p + 1], sinv[:, p:p + 1]))
            vec(f)
            t_sinv[p] = sDV.tok()

            etok_toks = []

            def f(sy, p=p, nbase=nbase):
                _w(sy, t_zu_prev)
                for g in range(8):
                    _w(sy, (sEX.sem, nbase + g + 1))
                    qS.bump(sy.dma_start(out=Etok[:, 4 * g:4 * g + 4, :],
                                         in_=Elm[:, 512 * g:512 * (g + 1)],
                                         transpose=True), 16)
                    etok_toks.append(qS.tok())
            block.sync(f)
            t_Etok_tr = qS.tok()
            t_Etok_tr_prev = t_Etok_tr

            lbase = sEX.n
            lmm_toks = []
            lbis = []
            for g in range(4):
                bi = nextbank()
                lbis.append(bi)

                def f(t, p=p, g=g, bi=bi, t_qevac=t_qevac, t_bdk_p=t_bdk_p):
                    _w(t, t_R4[bi])
                    if g == 0:
                        _w(t, t_qevac, t_bdk_p)
                    sPE.bump(t.matmul(rot4[bi][:], bdk[:, p, :],
                                      qTp[:, 512 * g:512 * (g + 1)],
                                      start=True, stop=True))
                block.tensor(f)
                lmm_toks.append(sPE.tok())
                t_R4[bi] = (sEX.sem, lbase + g + 1)
                hook_d()
            t_Lmm = sPE.tok()

            def f(s, p=p, lmm_toks=list(lmm_toks), lbis=list(lbis)):
                for g in range(4):
                    _w(s, lmm_toks[g])
                    sEX.bump(s.activation(
                        ELs[:, p, 512 * g:512 * (g + 1)], rot4[lbis[g]][:],
                        AF.Exp))
            act(f)
            assert sEX.n == lbase + 4
            t_Lexp = sEX.tok()

            def emit_dchain(p=p, t_Lexp=t_Lexp):
              nonlocal t_psRow_free, t_minv_mm, t_psI1_free, t_psI2_free
              psIbc = [psI1, psI2]
              for g in range(8):
                  yield
                  cols = slice(256 * g, 256 * (g + 1))

                  def f(t, p=p, g=g, cols=cols, t_Lexp=t_Lexp):
                      _w(t, t_Lexp, t_psRow_free)
                      sPE.bump(t.matmul(psRow[0:2, 0:256], c_sb["onescol2b"][:],
                                        ELs[:, p, cols], start=True, stop=True))
                  block.tensor(f)
                  mm_tok = sPE.tok()

                  def f(v, g=g, mm_tok=mm_tok, t_minv_mm=t_minv_mm):
                      _w(v, mm_tok, t_minv_mm)
                      v.drain()
                      v.tensor_copy(d_sb[:], psRow[0:2, 0:256])
                      v.drain()
                      sDV.bump(v.reciprocal(dinv[:], d_sb[:]))
                  vec(f)
                  t_psRow_free = sDV.tok()
                  t_dinv = sDV.tok()

                  yield
                  bnk = psIbc[g % 2]
                  t_bcf = t_psI1_free if g % 2 == 0 else t_psI2_free

                  def f(t, g=g, bnk=bnk, t_dinv=t_dinv, t_bcf=t_bcf):
                      _w(t, t_dinv, t_bcf)
                      sPE.bump(t.matmul(bnk[:, 0:2, :],
                                        c_sb["sel2f"][:],
                                        dinv[:], start=True, stop=True))
                  block.tensor(f)
                  t_minv_mm = sPE.tok()

                  def f(v, p=p, g=g, cols=cols, bnk=bnk, mm_tok=t_minv_mm):
                      _w(v, mm_tok)
                      sl = ELs[:, p, cols]
                      sDV.bump(
                          v.tensor_tensor(sl, sl,
                                          bnk[:, 0:2, :].rearrange("p a b -> p (a b)"),
                                          ALU.mult))
                  vec(f)
                  if g % 2 == 0:
                      t_psI1_free = sDV.tok()
                  else:
                      t_psI2_free = sDV.tok()
              t_ELs_done[p] = sDV.tok()

            if p == 3 and not DEFER_D3:
                for _ in emit_dchain():
                    pass
            elif p == 7:
                pending_d7[0] = emit_dchain()
            else:
                pending_d[0] = emit_dchain()

            def f(t, p=p, etok_toks=list(etok_toks),
                  vtok_toks=list(vtok_toks)):
                _w(t, t_mix_free)
                for tc in range(NSC):
                    if tc % 4 == 0:
                        _w(t, etok_toks[tc // 4], vtok_toks[tc // 4])
                    ins = t.matmul(psE, Etok[:, tc, :], vtok[:, tc, :],
                                   start=(tc == 0), stop=(tc == NSC - 1))
                sPE.bump(ins)
            block.tensor(f)
            t_zu_prev = sPE.tok()

            def f(v, p=p, t_zu=t_zu_prev):
                _w(v, t_zu, t_sinv[p])
                v.drain()
                sDV.bump(v.tensor_scalar(zun[:, p, :], psE,
                                         sinv[:, p:p + 1], None, ALU.mult))
            vec(f)
            t_mix_free = sDV.tok()
            t_psE_free_tok[0] = t_mix_free
            zgate1[0] = True

            if p == 0:
                tap("xT_early", xT[:], [128, NSC, NDC, 128], BF16, [t_qevac])
                tap("qTp0", qTp[:], [128, TOK], BF16, [t_qevac])
                tap("kTp0", kTp[:], [128, T], BF16, [t_kevac])
                tap("vTp0", vTp[:], [128, T], BF16, [t_vevac])
                tap("Elm0", Elm[:], [128, T], BF16, [t_nexp])
                tap("Etok0", Etok[:], [128, NSC, 128], BF16, [t_zu_prev])
                tap("vtok0", vtok[:], [128, NSC, 128], BF16, [t_zu_prev])
                tap("sinv0", sinv[:, 0:1], [128, 1], F32, [t_sinv[p]])
                tap("zun0", zun[:, 0, :], [128, 128], BF16, [t_mix_free])
                tap("ELs0", ELs[:, 0, :], [128, TOK], BF16, [t_ELs_done[0]])
                tap("bd0q", bdq[:], [128, NPAIR, 128], BF16, [t_bdk_p])
                tap("bd0k", bdk[:], [128, NPAIR, 128], BF16, [t_bdk_p])

            if p not in (3, 7):
                continue
            grp = p // 4
            g0 = 4 * grp

            def emit_pphase(grp=grp, g0=g0, zun_gate=None):
              nonlocal t_psI1_free, t_psI2_free, t_psRow_free, t_mix_free
              nonlocal t_yev, t_K4_free

              def f(t):
                  _w(t, t_bd_grp[grp], t_psI1_free)
                  for j in range(4):
                      ins = t.matmul(psI1[:, j, :], bdq[:, g0 + j, :],
                                     bdk[:, g0 + j, :], start=True, stop=True)
                  sPE.bump(ins)
              block.tensor(f)
              t_kl = sPE.tok()
              yield

              def f(v, t_kl=t_kl):
                  _w(v, t_kl)
                  v.tensor_reduce(kred[0:64, 0:4], psI1[0:64, :, 0:64], AX.X,
                                  ALU.max, negate=True)
                  sDV.bump(v.tensor_reduce(kred[64:128, 0:4],
                                           psI1[64:128, :, 64:128], AX.X,
                                           ALU.max, negate=True))
              vec(f)
              t_kmax = sDV.tok()
              yield

              def f(s, t_kmax=t_kmax, t_K4_free=t_K4_free):
                  _w(s, t_kmax, t_K4_free)
                  for j in range(4):
                      s.activation(K4[0:64, j, 0:64], psI1[0:64, j, 0:64],
                                   AF.Exp, bias=kred[0:64, j:j + 1],
                                   accum_out=kred[0:64, 4 + j:5 + j])
                      ins = s.activation(K4[64:128, j, 64:128],
                                         psI1[64:128, j, 64:128],
                                         AF.Exp, bias=kred[64:128, j:j + 1],
                                         accum_out=kred[64:128, 4 + j:5 + j])
                  sAC.bump(ins)
              act(f)
              t_kexp = sAC.tok()
              t_psI1_free = t_kexp
              yield

              def f(v, t_kexp=t_kexp):
                  _w(v, t_kexp)
                  v.reciprocal(kred[:, 4:8], kred[:, 4:8])
                  v.drain()
                  for j in range(4):
                      ins = v.tensor_scalar(K4[:, j, :], K4[:, j, :],
                                            kred[:, 4 + j:5 + j], None,
                                            ALU.mult)
                  sDV.bump(ins)
              vec(f)
              t_K = sDV.tok()
              yield

              def f(t, t_K=t_K):
                  _w(t, t_K, t_psRow_free)
                  sPE.bump(t.matmul(psRow[0:2, :], c_sb["onescol2f"][:],
                                    K4[:].rearrange("p a b -> p (a b)"),
                                    start=True, stop=True))
              block.tensor(f)
              t_cs = sPE.tok()
              yield

              def f(v, t_cs=t_cs):
                  _w(v, t_cs)
                  v.tensor_reduce(cden[:],
                                  psRow[0:2, :].rearrange(
                                      "p (a b) -> p a b", a=4),
                                  AX.X, ALU.max)
                  v.drain()
                  sDV.bump(v.reciprocal(cden[:], cden[:]))
              vec(f)
              t_cden = sDV.tok()
              t_psRow_free = t_cden
              yield

              def f(t, t_cden=t_cden, t_mf=t_mix_free):
                  _w(t, t_cden, t_mf)
                  sPE.bump(t.matmul(psSm[:, 0:4], c_sb["sel2f"][:], cden[:],
                                    start=True, stop=True))
              block.tensor(f)
              t_dnmm = sPE.tok()

              def f(v, t_dnmm=t_dnmm):
                  _w(v, t_dnmm)
                  sDV.bump(v.tensor_copy(denominv[:], psSm[:, 0:4]))
              vec(f)
              t_dn = sDV.tok()
              t_mix_free = t_dn
              yield

              def f(v, t_dn=t_dn, grp=grp):
                  _w(v, t_dn)
                  if grp == 1:
                      _w(v, t_pnv[0])
                  v.drain()
                  for j in range(4):
                      ins = v.tensor_scalar(TVg[:, j, :], K4[:, j, :],
                                            denominv[:, j:j + 1], None,
                                            ALU.mult)
                  sDV.bump(ins)
              vec(f)
              t_tv0 = sDV.tok()
              yield

              t_trev = t_tv0
              for j in range(4):
                  def f(t, j=j, t_trev=t_trev, t_mf=t_mix_free):
                      _w(t, t_K, t_trev, t_mf)
                      sPE.bump(t.transpose(psT128, K4[:, j, :],
                                           c_sb["eye128f"][:]))
                  block.tensor(f)
                  t_tr2 = sPE.tok()

                  def f(v, j=j, t_tr2=t_tr2):
                      _w(v, t_tr2)
                      v.tensor_copy(TK4[:, j, :], psT128)
                      sDV.bump(v.tensor_scalar(Vg[:, j, :], psT128,
                                               denominv[:, j:j + 1], None,
                                               ALU.mult))
                  vec(f)
                  t_trev = sDV.tok()
                  t_mix_free = t_trev
                  yield
              t_K4_free = t_trev
              t_state = t_trev

              for it in range(N_ITER):
                  last = it == N_ITER - 1

                  def f(t, t_state=t_state, tI1=t_psI1_free,
                        tI2=t_psI2_free):
                      _w(t, t_state, tI1, tI2)
                      for j in range(4):
                          t.matmul(psI1[:, j, :], TK4[:, j, :], Vg[:, j, :],
                                   start=True, stop=True)
                      for j in range(4):
                          ins = t.matmul(psI2[:, j, :], Vg[:, j, :],
                                         TK4[:, j, :], start=True, stop=True)
                      sPE.bump(ins)
                  block.tensor(f)
                  t_p1 = sPE.tok()

                  def f(v, t_p1=t_p1):
                      _w(v, t_p1)
                      v.scalar_tensor_tensor(
                          Tg[:], psI1[:], -1.0,
                          c_sb["eye7bd"][:].unsqueeze(1).to_broadcast(
                              [128, 4, 128]),
                          ALU.mult, ALU.add)
                      sDV.bump(v.tensor_copy(TP1g[:], psI2[:]))
                  vec(f)
                  t_T1 = sDV.tok()
                  t_psI1_free = t_T1
                  t_psI2_free = t_T1
                  yield

                  def f(t, t_T1=t_T1):
                      _w(t, t_T1)
                      for j in range(4):
                          ins = t.matmul(psI1[:, j, :], TP1g[:, j, :],
                                         Tg[:, j, :], start=True, stop=True)
                      sPE.bump(ins)
                  block.tensor(f)
                  t_p2 = sPE.tok()

                  def f(v, t_p2=t_p2):
                      _w(v, t_p2)
                      sDV.bump(v.scalar_tensor_tensor(
                          Tg[:], psI1[:], -1.0,
                          c_sb["eye15bd"][:].unsqueeze(1).to_broadcast(
                              [128, 4, 128]),
                          ALU.mult, ALU.add))
                  vec(f)
                  t_T2 = sDV.tok()
                  t_psI1_free = t_T2
                  yield

                  def f(t, t_T2=t_T2):
                      _w(t, t_T2)
                      for j in range(4):
                          ins = t.matmul(psI2[:, j, :], TP1g[:, j, :],
                                         Tg[:, j, :], start=True, stop=True)
                      sPE.bump(ins)
                  block.tensor(f)
                  t_p3 = sPE.tok()

                  def f(v, t_p3=t_p3):
                      _w(v, t_p3)
                      sDV.bump(v.scalar_tensor_tensor(
                          Tg[:], psI2[:], -1.0,
                          c_sb["eye13bd"][:].unsqueeze(1).to_broadcast(
                              [128, 4, 128]),
                          ALU.mult, ALU.add))
                  vec(f)
                  t_T3 = sDV.tok()
                  t_psI2_free = t_T3
                  yield

                  def f(t, t_T3=t_T3, last=last, tI1=t_psI1_free):
                      _w(t, t_T3, tI1)
                      if not last:
                          for j in range(4):
                              t.matmul(psI1[:, j, :], TVg[:, j, :],
                                       Tg[:, j, :], start=True, stop=True)
                      for j in range(4):
                          ins = t.matmul(psI2[:, j, :], Tg[:, j, :],
                                         TVg[:, j, :], start=True, stop=True)
                      sPE.bump(ins)
                  block.tensor(f)
                  t_vmm = sPE.tok()

                  def f(v, t_vmm=t_vmm, last=last):
                      _w(v, t_vmm)
                      if not last:
                          v.tensor_scalar(Vg[:], psI1[:], 0.25, None,
                                          ALU.mult)
                      sDV.bump(v.tensor_scalar(TVg[:], psI2[:], 0.25, None,
                                               ALU.mult))
                  vec(f)
                  t_state = sDV.tok()
                  t_psI1_free = t_state
                  t_psI2_free = t_state
                  yield

              while zun_gate is not None and not zun_gate[0]:
                  yield

              def f(t, t_state=t_state, tI1=t_psI1_free,
                    tE=t_psE_free_tok[0]):
                  _w(t, t_state, tI1, tE)
                  for j in range(4):
                      ins = t.matmul(psI1[:, j, :], TVg[:, j, :],
                                     zun[:, g0 + j, :], start=True, stop=True)
                  sPE.bump(ins)
              block.tensor(f)
              t_pnvmm = sPE.tok()

              def f(v, g0=g0, t_pnvmm=t_pnvmm):
                  _w(v, t_pnvmm)
                  v.tensor_copy(PNV[0:64, g0:g0 + 4, 0:64],
                                psI1[0:64, :, 0:64])
                  sDV.bump(v.tensor_copy(PNV[64:128, g0:g0 + 4, 64:128],
                                         psI1[64:128, :, 64:128]))
              vec(f)
              t_pnv[grp] = sDV.tok()
              t_psI1_free = t_pnv[grp]
              yield

              rotY = [psI1, psI2]
              for pp in range(g0, g0 + 4):
                  for g in range(4):
                      yb = g % 2
                      tfree = t_psI1_free if yb == 0 else t_psI2_free

                      def f(t, pp=pp, g=g, yb=yb, grp=grp, tfree=tfree):
                          _w(t, t_pnv[grp], t_ELs_done[pp], tfree)
                          out = rotY[yb][:].rearrange("q a b -> q (a b)")
                          sPE.bump(t.matmul(out, PNV[:, pp, :],
                                            ELs[:, pp,
                                                512 * g:512 * (g + 1)],
                                            start=True, stop=True))
                      block.tensor(f)
                      t_ymm = sPE.tok()

                      def f(v, pp=pp, g=g, yb=yb, t_ymm=t_ymm):
                          _w(v, t_ymm)
                          sDV.bump(v.tensor_copy(
                              yT[:, pp, 512 * g:512 * (g + 1)],
                              rotY[yb][:].rearrange("q a b -> q (a b)")))
                      vec(f)
                      if yb == 0:
                          t_psI1_free = sDV.tok()
                      else:
                          t_psI2_free = sDV.tok()
                      yield
              t_yev = sDV.tok()

            if grp == 0 and DEFER_P:
                pending_p[0] = emit_pphase()
            else:
                gd7 = pending_d7[0]
                pending_d7[0] = None
                gp1 = pending_p[0]
                pending_p[0] = None
                if gp1 is None:
                    gp1 = emit_pphase()
                live = [g for g in (gd7, gp1) if g is not None]
                while live:
                    for g in list(live):
                        if next(g, "END") == "END":
                            live.remove(g)

        tap("bdq", bdq[:], [128, NPAIR, 128], BF16, [t_yev])
        tap("bdk", bdk[:], [128, NPAIR, 128], BF16, [t_yev])
        tap("zun", zun[:], [128, NPAIR, 128], BF16, [t_yev])
        tap("PNV", PNV[:], [128, NPAIR, 128], BF16, [t_yev])
        tap("sinv", sinv[:], [128, NPAIR], F32, [t_yev])
        tap("ELs", ELs[:], [128, NPAIR, TOK], BF16, [t_yev])

        def f(g):
            _w(g, qS.tok(), qG.tok(), qC.tok(), qWs[0].tok(), qWs[1].tok())
        block.gpsimd(f)
        barrier()
        attn.__exit__(None, None, None)
        t_R4 = [None, None, None, None]

        ffn = contextlib.ExitStack()
        ffn.__enter__()
        psF = [ffn.enter_context(nc.psum_tensor(f"psF{i}", [128, 512], F32))
               for i in range(4)]
        bank8 = [psA, psB, psC, psD] + psF

        Wp_sb = ffn.enter_context(nc.sbuf_tensor("Wp_sb", [128, NDC, C], BF16))
        b2row = ffn.enter_context(nc.sbuf_tensor("b2row", [1, C], F32))
        b2b = ffn.enter_context(nc.sbuf_tensor("b2b", [128, C], F32))
        x0c = ffn.enter_context(nc.sbuf_tensor("x0c", [128, 4, C], F32))
        x1c = [ffn.enter_context(nc.sbuf_tensor(f"x1c{i}", [128, 4, C], F32))
               for i in range(2)]
        x1bc = ffn.enter_context(nc.sbuf_tensor("x1bc", [128, 4, C], BF16))
        x1Tc = [ffn.enter_context(
            nc.sbuf_tensor(f"x1Tc{i}", [128, 4, NDC, 128], BF16))
            for i in range(2)]
        hTc = ffn.enter_context(nc.sbuf_tensor("hTc", [128, NFC, CHT], BF16))
        w1s = [ffn.enter_context(nc.sbuf_tensor(f"w1s{i}", [128, NDC, 512],
                                                BF16)) for i in range(2)]
        w2s = [ffn.enter_context(nc.sbuf_tensor(f"w2s{i}", [128, C], BF16))
               for i in range(2)]
        r_sb = ffn.enter_context(nc.sbuf_tensor("r_sb", [8, 512], F32))
        rlog = ffn.enter_context(nc.sbuf_tensor("rlog", [128, 4, 8], F32))
        rex = ffn.enter_context(nc.sbuf_tensor("rex", [128, 4, 8], F32))
        rst = ffn.enter_context(nc.sbuf_tensor("rst", [128, 8], F32))
        p7c = ffn.enter_context(nc.sbuf_tensor("p7c", [128, 4], F32))
        stats = ffn.enter_context(nc.sbuf_tensor("stats", [128, 48], F32))
        mv = ffn.enter_context(nc.sbuf_tensor("mv", [128, 4, 2], F32))
        rstd = ffn.enter_context(nc.sbuf_tensor("rstd", [128, 4], F32))
        nmr = ffn.enter_context(nc.sbuf_tensor("nmr", [128, 4], F32))
        pre8 = ffn.enter_context(nc.sbuf_tensor("pre8", [128, 8, 512], F32))

        def f(s):
            qC.bump(s.dma_start(out=b2row[:], in_=b2.unsqueeze(0)), 16)
        act(f)
        t_b2row = qC.tok()

        def f(t):
            _w(t, t_b2row)
            t.matmul(psA[:], c_sb["ones1f"][:], b2row[0:1, 0:512],
                     start=True, stop=True)
            sPE.bump(t.matmul(psB[:], c_sb["ones1f"][:], b2row[0:1, 512:1024],
                              start=True, stop=True))
        block.tensor(f)
        t_b2mm = sPE.tok()

        def f(v):
            _w(v, t_b2mm)
            v.tensor_copy(b2b[:, 0:512], psA[:])
            sDV.bump(v.tensor_copy(b2b[:, 512:1024], psB[:]))
        vec(f)
        t_b2 = sDV.tok()

        def f(s):
            _w(s, t_scrp[0])
            qC.bump(s.dma_start(out=Wp_sb[:], in_=scr_Wp), 16)
        block.scalar(f)
        t_wp = qC.tok()

        t_x0 = [None]

        def load_x0(c, wait_tok):
            if c >= NCH:
                return

            def f(s, c=c, wait_tok=wait_tok):
                _w(s, wait_tok)
                t0 = CHT * c
                t_x0[0] = qX[0].bump(s.dma_start(
                    out=x0c[:],
                    in_=x_own[t0:t0 + CHT, :].rearrange(
                        "(tc tp) d -> tp tc d", tp=128)), 16)
            act(f)

        load_x0(0, None)

        t_w1 = [None, None]
        t_w1buf = [None, None]

        def load_w1(c, si):
            def f(sy, c=c, si=si):
                _w(sy, t_w1buf[si % 2], t_scr[0])
                t_w1[si % 2] = qF1[si % 2].bump(sy.dma_start(
                    out=w1s[si % 2][:], in_=scr_W1[si]), 16)
            block.sync(f)

        t_w2 = [None, None]
        t_w2buf = [None, None]

        def load_w2(c, f4):
            def f(s, c=c, f4=f4):
                _w(s, t_w2buf[f4 % 2], t_scr[0])
                t_w2[f4 % 2] = qF2[f4 % 2].bump(s.dma_start(
                    out=w2s[f4 % 2][:], in_=scr_W2[f4]), 16)
            block.scalar(f)

        t_banks_free = None
        t_store = [None, None]
        t_x1T_done = [None, None]
        t_x1bc_free = None
        t_ln_act_prev = None
        t_psRow2_free = None
        t_psSm2_free = None
        t_p7 = None

        pending_ln2 = [None]
        t_epi_sub = None
        t_pre_free = None

        for c in range(NCH):
            cb = c % 2

            load_w1(c, 0)
            load_w1(c, 1)
            load_w2(c, 0)
            load_w2(c, 1)

            t_wp_sub = [None] * 4

            def f(t, c=c, t_epi_sub=t_epi_sub):
                _w(t, t_wp, t_psRow2_free, t_psSm2_free)
                if c == 0:
                    _w(t, t_yev, t_b2)
                for sub in range(4):
                    if t_epi_sub is not None:
                        _w(t, t_epi_sub[sub])
                    tsl = slice(512 * c + 128 * sub,
                                512 * c + 128 * (sub + 1))
                    for dc in range(NDC):
                        for hh in range(2):
                            ins = t.matmul(
                                bank8[2 * sub + hh][:], yT[:, dc, tsl],
                                Wp_sb[:, dc, 512 * hh:512 * (hh + 1)],
                                start=(dc == 0), stop=(dc == NDC - 1))
                    t_wp_sub[sub] = sPE.bump(ins)
            block.tensor(f)

            def f(v, c=c, cb=cb, t_wp_sub=t_wp_sub):
                _w(v, t_x0[0], t_store[cb])
                for sub in range(4):
                    _w(v, t_wp_sub[sub])
                    for hh in range(2):
                        v.tensor_tensor(
                            x1c[cb][:, sub, 512 * hh:512 * (hh + 1)],
                            bank8[2 * sub + hh][:],
                            x0c[:, sub, 512 * hh:512 * (hh + 1)],
                            ALU.add)
                    v.drain()
                    v.bn_stats(stats[:, 12 * sub:12 * sub + 6],
                               x1c[cb][:, sub, 0:512].unsqueeze(1))
                    ins = v.bn_stats(stats[:, 12 * sub + 6:12 * sub + 12],
                                     x1c[cb][:, sub, 512:1024].unsqueeze(1))
                sDV.bump(ins)
            vec(f)
            t_x1pre = sDV.tok()
            t_banks_free = t_x1pre

            load_x0(c + 1, t_x1pre)

            def f(v, c=c, cb=cb, t_x1pre=t_x1pre):
                _w(v, t_x1pre, t_ln_act_prev)
                v.drain()
                for sub in range(4):
                    v.bn_aggr(mv[:, sub, :], stats[:, 12 * sub:12 * sub + 12])
                v.drain()
                sDV.bump(v.tensor_scalar(rstd[:], mv[:, :, 1], EPS, None,
                                         ALU.add))
            vec(f)
            t_bn = sDV.tok()

            def f(s, t_bn=t_bn):
                _w(s, t_bn)
                sAC.bump(s.activation(rstd[:], rstd[:], AF.Sqrt))
            act(f)
            t_sq = sAC.tok()

            def f(v, t_sq=t_sq):
                _w(v, t_sq)
                v.reciprocal(rstd[:], rstd[:])
                v.drain()
                sDV.bump(v.scalar_tensor_tensor(nmr[:], mv[:, :, 0], -1.0,
                                                rstd[:], ALU.mult, ALU.mult))
            vec(f)
            t_lns = sDV.tok()

            t_nbf = [None] * 4

            def f(v, c=c, cb=cb, t_lns=t_lns):
                _w(v, t_lns, t_x1bc_free)
                v.drain()
                for sub in range(4):
                    ins = v.tensor_scalar(x1bc[:, sub, :],
                                          x1c[cb][:, sub, :],
                                          rstd[:, sub:sub + 1],
                                          nmr[:, sub:sub + 1],
                                          ALU.mult, ALU.add)
                    t_nbf[sub] = sDV.bump(ins)
            vec(f)

            def f(sy, c=c, cb=cb, t_nbf=t_nbf):
                _w(sy, t_x1T_done[cb])
                for sub in range(4):
                    _w(sy, t_nbf[sub])
                    qS.bump(sy.dma_start(out=x1Tc[cb][:, sub, :, :],
                                         in_=x1bc[:, sub, :],
                                         transpose=True), 16)
            block.sync(f)
            t_x1T = qS.tok()
            t_x1bc_free = t_x1T

            def f(g, c=c, cb=cb, t_nbf3=t_nbf[3]):
                _w(g, t_nbf3)
                g.drain()
                for sub in range(4):
                    ins = g.tensor_scalar(x1c[cb][:, sub, :],
                                          x1c[cb][:, sub, :],
                                          rstd[:, sub:sub + 1],
                                          nmr[:, sub:sub + 1],
                                          ALU.mult, ALU.add)
                sPL.bump(ins)
            block.gpsimd(f)
            t_ln1f = sPL.tok()
            t_ln_act_prev = t_ln1f

            if pending_ln2[0] is not None:
                pending_ln2[0]()
                pending_ln2[0] = None

            def emit_router(c=c, cb=cb, t_x1T_tok=t_x1T):
                nonlocal t_psRow2_free, t_psSm2_free, t_p7
                def f(t, c=c, cb=cb):
                    _w(t, t_x1T_tok, t_banks_free)
                    for dc in range(NDC):
                        ins = t.matmul(psF[2][0:8, :], wr_sb[:, dc, :],
                                       x1Tc[cb][:, :, dc, :],
                                       start=(dc == 0), stop=(dc == NDC - 1))
                    sPE.bump(ins)
                block.tensor(f)
                t_rmm = sPE.tok()

                def f(v, t_rmm=t_rmm):
                    _w(v, t_rmm)
                    sDV.bump(v.tensor_copy(r_sb[:], psF[2][0:8, :]))
                vec(f)
                t_rev = sDV.tok()
                t_psRow2_free = t_rev

                t_rtr_ev = None
                for u in range(4):
                    def f(t, u=u, t_rev=t_rev, t_rtr_ev=t_rtr_ev):
                        _w(t, t_rev, t_rtr_ev)
                        sPE.bump(t.transpose(psF[3][:, 0:8],
                                             r_sb[:, 128 * u:128 * (u + 1)],
                                             c_sb["eye8f"][:]))
                    block.tensor(f)
                    t_rtr = sPE.tok()

                    def f(v, u=u, t_rtr=t_rtr):
                        _w(v, t_rtr)
                        sDV.bump(v.tensor_copy(rlog[:, u, :], psF[3][:, 0:8]))
                    vec(f)
                    t_rtr_ev = sDV.tok()
                t_psSm2_free = t_rtr_ev

                def f(v, t_rtr_ev=t_rtr_ev):
                    _w(v, t_rtr_ev)
                    v.drain()
                    v.tensor_reduce(rst[:, 0:4], rlog[:], AX.X, ALU.max)
                    v.drain()
                    sDV.bump(v.tensor_tensor(
                        rex[:], rlog[:],
                        rst[:, 0:4].unsqueeze(2).to_broadcast([128, 4, 8]),
                        ALU.subtract))
                vec(f)
                t_rsub = sDV.tok()

                def f(s, t_rsub=t_rsub):
                    _w(s, t_rsub)
                    sAC.bump(s.activation(rex[:], rex[:], AF.Exp))
                act(f)
                t_rexp = sAC.tok()

                def f(v, t_rexp=t_rexp):
                    _w(v, t_rexp)
                    v.tensor_reduce(rst[:, 4:8], rex[:], AX.X, ALU.add)
                    v.drain()
                    v.reciprocal(rst[:, 4:8], rst[:, 4:8])
                    v.drain()
                    sDV.bump(v.tensor_tensor(p7c[:], rex[:, :, 7], rst[:, 4:8],
                                             ALU.mult))
                vec(f)
                t_p7 = sDV.tok()

            rot6w = [psA, psB, psC, psD, psF[0], psF[1]]
            t_R6 = [t_banks_free] * 6
            mm4 = [None] * 4
            for si in range(8):
                if 1 <= si < 7:
                    load_w1(c, si + 1)
                if si == 1:
                    emit_router()
                for fc in range(4):
                    f4 = 4 * si + fc
                    bi = f4 % 6

                    def f(t, c=c, cb=cb, si=si, fc=fc, bi=bi,
                          t_x1T=t_x1T, t_R6b=t_R6[bi]):
                        _w(t, t_w1[si % 2])
                        if si == 0 and fc == 0:
                            _w(t, t_x1T)
                        _w(t, t_R6b)
                        ps = rot6w[bi]
                        for dc in range(NDC):
                            ins = t.matmul(
                                ps[:],
                                w1s[si % 2][:, dc, 128 * fc:128 * (fc + 1)],
                                x1Tc[cb][:, :, dc, :],
                                start=(dc == 0), stop=(dc == NDC - 1))
                        sPE.bump(ins)
                    block.tensor(f)
                    mm4[fc] = sPE.tok()

                    if fc == 3:
                        def f(s, f4=f4, mm4tok=mm4[3]):
                            _w(s, mm4tok)
                            for k_ in range(4):
                                fx = f4 - 3 + k_
                                ins = s.activation(
                                    hTc[:, fx, :], rot6w[fx % 6][:],
                                    AF.Gelu,
                                    bias=b1_sb[:, fx:fx + 1])
                                t_R6[fx % 6] = sAC.bump(ins)
                        act(f)
                t_w1buf[si % 2] = sPE.tok()
            t_x1T_done[cb] = sPE.tok()
            t_gelu = sAC.tok()

            t_w2bank = [None] * 8
            for f4 in range(NFC):
                if 1 <= f4 < NFC - 1:
                    load_w2(c, f4 + 1)

                def f(t, c=c, f4=f4):
                    _w(t, t_w2[f4 % 2], t_gelu)
                    if f4 == 0:
                        _w(t, t_psRow2_free, t_psSm2_free)
                    for sub in range(4):
                        for hh in range(2):
                            ins = t.matmul(
                                bank8[2 * sub + hh][:],
                                hTc[:, f4, 128 * sub:128 * (sub + 1)],
                                w2s[f4 % 2][:, 512 * hh:512 * (hh + 1)],
                                start=(f4 == 0), stop=(f4 == NFC - 1))
                            if f4 == NFC - 1:
                                t_w2bank[2 * sub + hh] = sPE.bump(ins)
                    if f4 != NFC - 1:
                        sPE.bump(ins)
                block.tensor(f)
                t_w2buf[f4 % 2] = sPE.tok()
            t_w2_done = sPE.tok()

            t_pre = [None] * 8

            def f(g, c=c, cb=cb, t_p7=t_p7, t_pre_free=t_pre_free):
                _w(g, t_p7, t_b2, t_pre_free)
                g.drain()
                for k in range(8):
                    sub, hh = divmod(k, 2)
                    g.tensor_scalar(pre8[:, k, :],
                                    b2b[:, 512 * hh:512 * (hh + 1)],
                                    p7c[:, sub:sub + 1], None, ALU.mult)
                g.drain()
                for k in range(8):
                    sub, hh = divmod(k, 2)
                    ins = g.tensor_tensor(
                        pre8[:, k, :], pre8[:, k, :],
                        x1c[cb][:, sub, 512 * hh:512 * (hh + 1)], ALU.add)
                    t_pre[k] = sPL.bump(ins)
            block.gpsimd(f)

            t_epi_sub = [None] * 4

            def f(v, c=c, cb=cb, t_pre=t_pre, t_w2bank=t_w2bank,
                  t_epi_sub=t_epi_sub, t_p7=t_p7):
                _w(v, t_p7)
                v.drain()
                for k in range(8):
                    sub, hh = divmod(k, 2)
                    _w(v, t_w2bank[k], t_pre[k])
                    ins = v.scalar_tensor_tensor(
                        x1c[cb][:, sub, 512 * hh:512 * (hh + 1)],
                        bank8[k][:], p7c[:, sub:sub + 1],
                        pre8[:, k, :], ALU.mult, ALU.add)
                    if k % 2 == 1:
                        t_epi_sub[sub] = sDV.bump(ins)
            vec(f)
            t_ffn_done = sDV.tok()
            t_banks_free = t_ffn_done
            t_pre_free = t_ffn_done

            def emit_ln2(c=c, cb=cb, t_ffn_done=t_ffn_done):
                nonlocal t_ln_act_prev
                def f(v, c=c, cb=cb):
                  _w(v, t_ffn_done, t_ln_act_prev)
                  v.drain()
                  for sub in range(4):
                      v.bn_stats(stats[:, 12 * sub:12 * sub + 6],
                                 x1c[cb][:, sub, 0:512].unsqueeze(1))
                      v.bn_stats(stats[:, 12 * sub + 6:12 * sub + 12],
                                 x1c[cb][:, sub, 512:1024].unsqueeze(1))
                  v.drain()
                  for sub in range(4):
                      v.bn_aggr(mv[:, sub, :], stats[:, 12 * sub:12 * sub + 12])
                  v.drain()
                  sDV.bump(v.tensor_scalar(rstd[:], mv[:, :, 1], EPS, None,
                                           ALU.add))
                vec(f)
                t_bn2 = sDV.tok()

                def f(s, t_bn2=t_bn2):
                  _w(s, t_bn2)
                  sAC.bump(s.activation(rstd[:], rstd[:], AF.Sqrt))
                act(f)
                t_sq2 = sAC.tok()

                def f(v, t_sq2=t_sq2):
                  _w(v, t_sq2)
                  v.reciprocal(rstd[:], rstd[:])
                  v.drain()
                  sDV.bump(v.scalar_tensor_tensor(nmr[:], mv[:, :, 0], -1.0,
                                                  rstd[:], ALU.mult, ALU.mult))
                vec(f)
                t_lns2 = sDV.tok()

                def f(g, c=c, cb=cb, t_lns2=t_lns2):
                  _w(g, t_lns2)
                  g.drain()
                  for sub in range(4):
                      ins = g.tensor_scalar(x1c[cb][:, sub, :],
                                            x1c[cb][:, sub, :],
                                            rstd[:, sub:sub + 1],
                                            nmr[:, sub:sub + 1],
                                            ALU.mult, ALU.add)
                  sPL.bump(ins)
                block.gpsimd(f)
                t_ln2 = sPL.tok()
                t_ln_act_prev = t_ln2

                def f(s, c=c, cb=cb, t_ln2=t_ln2):
                  _w(s, t_ln2)
                  t0 = CHT * c
                  t_store[cb] = qST.bump(s.dma_start(
                      out=out_h[t0:t0 + CHT, :].rearrange(
                          "(tc tp) d -> tp tc d", tp=128),
                      in_=x1c[cb][:]), 16)
                act(f)

            pending_ln2[0] = emit_ln2

        if pending_ln2[0] is not None:
            pending_ln2[0]()
            pending_ln2[0] = None

        def f(s):
            _w(s, qST.tok(), qC.tok(), qX[0].tok(), qX[1].tok())
        act(f)

        def f(g):
            _w(g, qG.tok(), qF1[0].tok(), qF1[1].tok(), qF2[0].tok(),
               qF2[1].tok(), qV.tok())
        block.gpsimd(f)

        ffn.__exit__(None, None, None)
    return nc


_CACHE = {}


def kernel(L0_attn_x, L0_attn_W_attn, L0_attn_W_proj, L0_ffn_Wr, L0_ffn_W1,
           L0_ffn_b1, L0_ffn_W2, L0_ffn_b2):
    x = np.ascontiguousarray(np.asarray(L0_attn_x, dtype=np.float32))
    W_attn = np.ascontiguousarray(np.asarray(L0_attn_W_attn, dtype=np.float32))
    W_proj = np.ascontiguousarray(np.asarray(L0_attn_W_proj, dtype=np.float32))
    WrT = np.ascontiguousarray(np.asarray(L0_ffn_Wr, dtype=np.float32).T)
    W1 = np.ascontiguousarray(np.asarray(L0_ffn_W1, dtype=np.float32)[NEXP - 1])
    b1v = np.ascontiguousarray(np.asarray(L0_ffn_b1, dtype=np.float32)[NEXP - 1])
    W2 = np.ascontiguousarray(np.asarray(L0_ffn_W2, dtype=np.float32)[NEXP - 1])
    b2v = np.ascontiguousarray(np.asarray(L0_ffn_b2, dtype=np.float32)[NEXP - 1])

    if "nc" not in _CACHE:
        _CACHE["nc"] = build_nc()
    nc = _CACHE["nc"]

    in_maps = []
    for core in range(8):
        b, h = divmod(core, 2)
        m = {
            "x_own": np.ascontiguousarray(x[b, h * TOK:(h + 1) * TOK]),
            "x_par": np.ascontiguousarray(x[b, (1 - h) * TOK:(2 - h) * TOK]),
            "W_attn": W_attn, "W_proj": W_proj, "WrT": WrT,
            "W1": W1, "b1": np.ascontiguousarray(b1v.reshape(NFC, 128).T),
            "W2": W2, "b2": b2v,
        }
        m.update(_host_consts(h))
        in_maps.append(m)

    res = run_bass_kernel_spmd(nc, in_maps, core_ids=list(range(8)))
    out = np.concatenate([res.results[i]["out"] for i in range(8)], axis=0)
    return out.reshape(B, T, C)
